# version 02
# baseline (speedup 1.0000x reference)
import numpy as np
import ml_dtypes
from contextlib import ExitStack

import concourse.bass as bass
import concourse.mybir as mybir
from concourse.bass_utils import run_bass_kernel_spmd

F32 = mybir.dt.float32
BF16 = mybir.dt.bfloat16
I32 = mybir.dt.int32
U8 = mybir.dt.uint8
I8 = mybir.dt.int8
OP = mybir.AluOpType
AF = mybir.ActivationFunctionType
AX = mybir.AxisListType
bf16 = ml_dtypes.bfloat16

HID = 4096
INTER = 11008
NCORES = 8
TPC = 1024
NSLAB = 32
NPAIR = 86
NCH = 172
DN_KCS = [8] * 10 + [6]
NPIECE = 22
N_SUBLAYERS = 3

_p = np.arange(64)
PI128 = np.concatenate([8 * (_p // 4) + 2 * (_p % 4),
                        8 * (_p // 4) + 2 * (_p % 4) + 1])


def perm(n):
    i = np.arange(n)
    return (i // 128) * 128 + PI128[i % 128]


PERMH = perm(HID)
PERMI = perm(INTER)
PERMQ = perm(1024)


def stage_weights(inp):
    d = {}
    f32 = np.float32

    A = np.asarray(inp["L0_ffn_qweight_gu"])[PERMH]
    gate_w = A[:, :INTER // 8].reshape(HID, NPAIR, 16)
    up_w = A[:, INTER // 8:].reshape(HID, NPAIR, 16)
    ch = np.empty((NCH, HID, 16), dtype=np.int32)
    ch[0::2] = gate_w.transpose(1, 0, 2)
    ch[1::2] = up_w.transpose(1, 0, 2)
    d["qwgu"] = np.ascontiguousarray(
        ch.reshape(NCH, NSLAB, 128, 16).transpose(0, 2, 1, 3)).reshape(NCH, 128, 512)

    S = np.asarray(inp["L0_ffn_scales_gu"]).astype(f32)
    gate_s = S[:, :INTER].reshape(32, NPAIR, 128)[:, :, PI128]
    up_s = S[:, INTER:].reshape(32, NPAIR, 128)[:, :, PI128]
    srows = np.empty((NCH, 32, 128), dtype=f32)
    srows[0::2] = gate_s.transpose(1, 0, 2)
    srows[1::2] = up_s.transpose(1, 0, 2)
    d["sgu_rows"] = np.ascontiguousarray(srows.reshape(NCH, 4096))
    d["sgu_g"] = np.ascontiguousarray(
        srows.transpose(1, 0, 2).reshape(32, NCH * 128))
    d["qzgu"] = np.ascontiguousarray(np.asarray(inp["L0_ffn_qzeros_gu"]))

    Bm = np.asarray(inp["L0_ffn_qweight_dn"])[PERMI]
    Bk = Bm.reshape(NPAIR, 128, 32, 16)
    blocks = []
    s0 = 0
    for ns in DN_KCS:
        Ak = Bk[s0:s0 + ns]
        for hcg in range(8):
            blk = Ak[:, :, 4 * hcg:4 * hcg + 4, :]
            blocks.append(blk.transpose(1, 0, 2, 3).reshape(128, ns * 64))
        s0 += ns
    d["qwdn"] = np.ascontiguousarray(np.concatenate(blocks, axis=1))

    Sd = np.asarray(inp["L0_ffn_scales_dn"]).astype(f32)
    Sd_int = Sd.reshape(NPAIR, 32, 128)[:, :, PI128]
    rows = np.zeros((len(DN_KCS) * 8, 4096), dtype=f32)
    s0 = 0
    for kc, ns in enumerate(DN_KCS):
        for hcg in range(8):
            blk = Sd_int[s0:s0 + ns, 4 * hcg:4 * hcg + 4, :]
            rows[kc * 8 + hcg, :ns * 512] = blk.reshape(-1)
        s0 += ns
    d["sdn_rows"] = rows
    d["sdn_g"] = np.ascontiguousarray(Sd_int.reshape(NPAIR, 4096))
    d["qzdn"] = np.ascontiguousarray(np.asarray(inp["L0_ffn_qzeros_dn"]))

    Wq = np.asarray(inp["L0_attn_Wq"]).astype(f32)
    sel = np.concatenate([np.arange(4 * h * 128, 4 * h * 128 + 128)
                          for h in range(8)])
    wqT = Wq[sel][PERMQ][:, PERMH].T
    d["wq"] = np.ascontiguousarray(
        wqT.reshape(NSLAB, 128, 8, 128).transpose(2, 1, 0, 3)).reshape(8, 128, 4096)
    bq = np.asarray(inp["L0_attn_bq"]).astype(f32)[sel][PERMQ]
    d["bq8"] = np.ascontiguousarray(bq.reshape(8, 128).T)
    M = np.asarray(inp["L0_attn_M"]).astype(f32)
    d["m8"] = np.ascontiguousarray(M[:, PI128][:, :, PI128])
    z = np.asarray(inp["L0_attn_z"]).astype(f32)
    d["z8"] = np.ascontiguousarray(z[:, PI128].T)
    Wo = np.asarray(inp["L0_attn_Wo"]).astype(f32)
    woT = Wo.T[perm(4096)][:, PERMH]
    d["wo"] = np.ascontiguousarray(
        woT.reshape(32, 128, 32, 128).transpose(2, 1, 0, 3)).reshape(32, 128, 4096)

    d["ones_col"] = np.ones((128, 1), dtype=f32)
    d["ones_row"] = np.ones((1, 128), dtype=f32)
    d["ident"] = np.eye(128, dtype=f32)
    return d


class Sem:
    def __init__(self, h):
        self.h = h
        self.n = 0


class B:

    def __init__(self, nc, stack):
        self.nc = nc
        self.stack = stack
        self.q = {e: [] for e in ("sync", "gpsimd", "scalar", "vector", "tensor")}
        self.sems = {}
        for s in ("ld", "cst", "dbl", "acc", "dv", "pe", "ac"):
            self.sems[s] = Sem(stack.enter_context(nc.semaphore("s_" + s)))

    def sb(self, name, p, cols, dt):
        return self.stack.enter_context(self.nc.sbuf_tensor(name, [p, cols], dt))

    def dma(self, eng, sem, out, in_, **kw):
        s = self.sems[sem]
        s.n += 16
        self.q[eng].append(lambda e, out=out, in_=in_, h=s.h, kw=kw:
                           e.dma_start(out=out, in_=in_, **kw).then_inc(h, 16))
        return s.n

    def wait(self, eng, sem, v):
        if v <= 0:
            return
        if sem in ("ld", "cst", "dbl", "acc"):
            v = self.sems[sem].n
        h = self.sems[sem].h
        self.q[eng].append(lambda e, h=h, v=v: e.wait_ge(h, v))

    def op(self, eng, sem, fn):
        s = self.sems[sem]
        s.n += 1
        self.q[eng].append(lambda e, fn=fn, h=s.h: fn(e).then_inc(h, 1))
        return s.n

    def opn(self, eng, fn):
        self.q[eng].append(fn)

    def fence(self, eng, sem):
        s = self.sems[sem]
        s.n += 1
        if eng == "vector":
            self.q[eng].append(lambda e, h=s.h: e.tensor_copy(
                out=self.fbuf[0:1, 0:2], in_=self.fbuf[0:1, 2:4]).then_inc(h, 1))
        elif eng == "scalar":
            self.q[eng].append(lambda e, h=s.h: e.copy(
                out=self.fbuf[0:1, 4:6], in_=self.fbuf[0:1, 6:8]).then_inc(h, 1))
        else:
            raise ValueError(eng)
        return s.n

    def opf(self, eng, sem, fn):
        self.opn(eng, lambda e, fn=fn: fn(e))
        return self.fence(eng, sem)

    def barrier(self, eng, sems):
        for s in sems:
            self.wait(eng, s, self.sems[s].n)


def build_nc(n_sub=N_SUBLAYERS, tap=None):
    nc = bass.Bass("TRN2", target_bir_lowering=False)
    ap = {}
    for name, shape, dt in [
            ("xin", [HID, TPC], F32),
            ("qwgu", [NCH, 128, 512], I32),
            ("sgu_rows", [NCH, 4096], F32),
            ("sgu_g", [32, NCH * 128], F32),
            ("qzgu", [32, 2752], I32),
            ("qwdn", [128, NPAIR * 512], I32),
            ("sdn_rows", [len(DN_KCS) * 8, 4096], F32),
            ("sdn_g", [NPAIR, 4096], F32),
            ("qzdn", [NPAIR, 512], I32),
            ("wq", [8, 128, 4096], F32),
            ("bq8", [128, 8], F32),
            ("m8", [8, 128, 128], F32),
            ("z8", [128, 8], F32),
            ("wo", [32, 128, 4096], F32),
            ("ones_col", [128, 1], F32),
            ("ones_row", [1, 128], F32),
            ("ident", [128, 128], F32)]:
        ap[name] = nc.dram_tensor(name, shape, dt, kind="ExternalInput").ap()
    xout = nc.dram_tensor("xout", [HID, TPC], F32, kind="ExternalOutput").ap()
    zgu_rows = nc.dram_tensor("zgu_rows", [NCH, 4096], U8, kind="Internal").ap()
    zdn_rows = nc.dram_tensor("zdn_rows", [len(DN_KCS) * 8, 4096], U8,
                              kind="Internal").ap()

    stack = ExitStack()
    with stack:
        b = B(nc, stack)
        sb = b.sb
        y = sb("y", 128, NSLAB * 1024, BF16)
        bq8 = sb("bq8s", 128, 8, F32)
        z8 = sb("z8s", 128, 8, BF16)
        m8 = sb("m8s", 128, 8 * 128, BF16)
        identb = sb("identb", 128, 128, BF16)
        identf = sb("identf", 128, 128, F32)
        onec = sb("onec", 128, 1, BF16)
        onerf = sb("onerf", 1, 128, F32)
        srow = sb("srow", 1, 4096, BF16)
        srow2 = sb("srow2", 1, 4096, BF16)
        srow2z = sb("srow2z", 1, 4096, U8)
        srowdz = sb("srowdz", 1, 4096, U8)
        mrow = sb("mrow", 1, 1024, F32)
        rrow = sb("rrow", 1, 1024, F32)
        statc = sb("statc", 128, 64, F32)
        arena_h = sb("arena", 128, 106496, U8)
        arena = arena_h[:]

        def carve(off, nbytes, dt, p=128):
            return arena[:, off:off + nbytes].bitcast(dt)[0:p, :]

        qwg = carve(0, 4096, I32)
        wi8 = carve(4096, 4096, U8)
        wdqg = carve(8192, 16384, BF16)
        sbg = carve(24576, 8192, BF16)
        hbuf = carve(32768, 16384, BF16)
        qwd = carve(49152, 4096, I32)
        wdqd = carve(53248, 16384, BF16)
        sbd = carve(69632, 16384, BF16)
        dnst = carve(86016, 4096, BF16)
        zbg = carve(90112, 4096, U8)
        silu_sb = carve(94208, 4096, BF16)
        xsl = carve(53248, 8192, F32)
        xbfs = carve(61440, 4096, BF16)
        zbd = carve(102400, 4096, U8)
        xsqs = carve(65536, 4096, BF16)
        wsub = carve(98304, 4096, I8)
        ln_tmp = carve(86016, 4096, F32)
        sq = carve(0, 16384, BF16)
        retr = carve(16384, 16384, BF16)
        wtile = carve(32768, 16384, BF16)
        qscr = carve(49152, 8192, F32)
        rcp = carve(57344, 512, F32)
        b.fbuf = sb("fbuf", 1, 8, F32)

        pA = stack.enter_context(nc.psum_tensor("pA", [128, 1024], F32))
        pB = stack.enter_context(nc.psum_tensor("pB", [128, 1024], F32))
        pC = stack.enter_context(nc.psum_tensor("pC", [128, 1024], F32))
        pD = stack.enter_context(nc.psum_tensor("pD", [128, 1024], F32))

        acc_total = [0]
        v_acc_region = {}
        evac_slot = [0, 0]
        dnst_free = [0, 0]

        b.opn("vector", lambda e: e.memset(b.fbuf[0:1, 0:8], 0.0))
        v_d2d = b.dma("sync", "ld", xout, ap["xin"])
        b.dma("gpsimd", "cst", onec[:], ap["ones_col"])
        b.dma("sync", "ld", onerf[:], ap["ones_row"])
        b.dma("gpsimd", "cst", identb[:], ap["ident"])
        b.dma("sync", "ld", identf[:], ap["ident"])
        b.dma("sync", "ld", bq8[:], ap["bq8"])
        b.dma("gpsimd", "cst", z8[:], ap["z8"])
        v_m8 = 0
        for h in range(8):
            v_m8 = b.dma("gpsimd", "cst", m8[:, h * 128:(h + 1) * 128], ap["m8"][h])

        qzg_sb = carve(53248, 16384, I32, p=32)
        vz = b.dma("sync", "ld", qzg_sb[:, 0:2752], ap["qzgu"])
        zb = carve(0, 32768, U8)
        qzv = qzg_sb.bitcast(U8)
        b.wait("vector", "ld", vz)
        vu = 0
        for gate in range(2):
            base = gate * 5504
            in_v = qzv[0:32, base:base + 86 * 64].rearrange(
                "g (i bb) -> g i bb", bb=64)
            for hi in range(2):
                out_v = zb[0:32, 0:22016].rearrange(
                    "g (i blk sub) -> g i blk sub", blk=4, sub=64)[
                    :, :, gate * 2 + hi, :]
                if hi == 0:
                    vu = b.op("vector", "dv",
                              lambda e, o=out_v, iv=in_v: e.tensor_scalar(
                                  out=o, in0=iv, scalar1=15, scalar2=None,
                                  op0=OP.bitwise_and))
                else:
                    vu = b.op("vector", "dv",
                              lambda e, o=out_v, iv=in_v: e.tensor_scalar(
                                  out=o, in0=iv, scalar1=4, scalar2=15,
                                  op0=OP.logical_shift_right,
                                  op1=OP.bitwise_and))
        vu = b.fence("vector", "dv")
        b.wait("sync", "dv", vu)
        v_store = 0
        for q in range(4):
            v_store = b.dma("sync", "ld",
                            zgu_rows.rearrange("c (g p) -> g c p", g=32)[
                                q * 8:(q + 1) * 8, :, :],
                            zb[q * 8:(q + 1) * 8, 0:22016].rearrange(
                                "g (c p) -> g c p", p=128))
        qzd_sb = carve(53248, 16384, I32, p=NPAIR)
        vz3 = b.dma("sync", "ld", qzd_sb[:, 0:512], ap["qzdn"])
        b.wait("vector", "ld", vz3)
        b.wait("vector", "ld", v_store)
        qzv2 = qzd_sb[:].bitcast(U8)
        vu = 0
        for hi in range(2):
            in_v = qzv2[0:NPAIR, 0:2048].rearrange("g (hc bb) -> g hc bb", bb=64)
            out_v = zb[0:NPAIR, 0:4096].rearrange(
                "g (hc blk sub) -> g hc blk sub", blk=2, sub=64)[:, :, hi, :]
            if hi == 0:
                vu = b.op("vector", "dv", lambda e, o=out_v, iv=in_v: e.tensor_scalar(
                    out=o, in0=iv, scalar1=15, scalar2=None, op0=OP.bitwise_and))
            else:
                vu = b.op("vector", "dv", lambda e, o=out_v, iv=in_v: e.tensor_scalar(
                    out=o, in0=iv, scalar1=4, scalar2=15,
                    op0=OP.logical_shift_right, op1=OP.bitwise_and))
        v_init_done = b.fence("vector", "dv")
        b.wait("sync", "dv", v_init_done)
        s0 = 0
        v_ztab = 0
        for kc, ns in enumerate(DN_KCS):
            v_ztab = b.dma("sync", "ld",
                           bass.AP(zdn_rows.tensor, kc * 8 * 4096,
                                   [[512, ns], [4096, 8], [128, 4], [1, 128]]),
                           zb[s0:s0 + ns, 0:4096].rearrange("g (hc p) -> g hc p", p=128))
            s0 += ns

        def emit_ln(src_dram, want_Y):
            pe_done = [0, 0]
            v_xb = [0, 0]
            v_xq = [0, 0]
            b.wait("sync", "acc", acc_total[0])
            b.barrier("sync", ["ld", "dv", "ac"])
            b.barrier("tensor", ["dv", "ac"])
            b.barrier("vector", ["pe", "ac"])
            b.barrier("scalar", ["pe", "dv"])
            for g in range(NSLAB):
                bu = g % 2
                b.wait("sync", "dv", v_xb[bu])
                b.wait("sync", "ac", v_xq[bu])
                vld = b.dma("sync", "ld", xsl[:, bu * 1024:(bu + 1) * 1024],
                            src_dram[g * 128:(g + 1) * 128, :])
                b.wait("vector", "ld", vld)
                b.wait("vector", "pe", pe_done[bu])
                v_xb[bu] = b.opf("vector", "dv", lambda e, bu=bu: e.tensor_copy(
                    out=xbfs[:, bu * 1024:(bu + 1) * 1024],
                    in_=xsl[:, bu * 1024:(bu + 1) * 1024]))
                b.wait("scalar", "ld", vld)
                b.wait("scalar", "pe", pe_done[bu])
                v_xq[bu] = b.opf("scalar", "ac", lambda e, bu=bu: e.activation(
                    out=xsqs[:, bu * 1024:(bu + 1) * 1024],
                    in_=xsl[:, bu * 1024:(bu + 1) * 1024], func=AF.Square))
                b.wait("tensor", "dv", v_xb[bu])
                b.wait("tensor", "ac", v_xq[bu])
                for t in range(8):
                    b.opn("tensor", lambda e, t=t, g=g, bu=bu: e.matmul(
                        pC[:, t * 64 + g:t * 64 + g + 1],
                        xbfs[:, bu * 1024 + t * 128:bu * 1024 + (t + 1) * 128],
                        onec[:], start=True, stop=True))
                for t in range(8):
                    if t < 7:
                        b.opn("tensor", lambda e, t=t, g=g, bu=bu: e.matmul(
                            pC[:, t * 64 + 32 + g:t * 64 + 32 + g + 1],
                            xsqs[:, bu * 1024 + t * 128:bu * 1024 + (t + 1) * 128],
                            onec[:], start=True, stop=True))
                    else:
                        pe_done[bu] = b.op("tensor", "pe",
                                           lambda e, t=t, g=g, bu=bu: e.matmul(
                                               pC[:, t * 64 + 32 + g:t * 64 + 33 + g],
                                               xsqs[:, bu * 1024 + t * 128:
                                                    bu * 1024 + (t + 1) * 128],
                                               onec[:], start=True, stop=True))
            v_stats_pe = pe_done[(NSLAB - 1) % 2]
            b.wait("vector", "pe", v_stats_pe)
            vv = 0
            for t in range(8):
                vv = b.op("vector", "dv", lambda e, t=t: e.reduce_sum(
                    out=statc[:, t:t + 1], in_=pC[:, t * 64:t * 64 + 32], axis=AX.X))
                vv = b.op("vector", "dv", lambda e, t=t: e.reduce_sum(
                    out=statc[:, 8 + t:9 + t], in_=pC[:, t * 64 + 32:t * 64 + 64],
                    axis=AX.X))
            b.wait("vector", "dv", vv)
            vv = b.op("vector", "dv", lambda e: e.tensor_scalar(
                out=statc[:, 16:24], in0=statc[:, 0:8], scalar1=1.0 / HID,
                scalar2=None, op0=OP.mult))
            vv = b.op("vector", "dv", lambda e: e.tensor_scalar(
                out=statc[:, 24:32], in0=statc[:, 0:8], scalar1=1.0 / 32.0,
                scalar2=None, op0=OP.mult))
            b.wait("vector", "dv", vv)
            vv = b.op("vector", "dv", lambda e: e.tensor_mul(
                out=statc[:, 32:40], in0=statc[:, 16:24], in1=statc[:, 16:24]))
            b.wait("vector", "dv", vv)
            vv = b.op("vector", "dv", lambda e: e.scalar_tensor_tensor(
                out=statc[:, 40:48], in0=statc[:, 8:16], scalar=1.0 / HID,
                in1=statc[:, 32:40], op0=OP.mult, op1=OP.subtract))
            b.wait("vector", "dv", vv)
            vv = b.opf("vector", "dv", lambda e: e.tensor_scalar(
                out=statc[:, 40:48], in0=statc[:, 40:48], scalar1=1e-5,
                scalar2=None, op0=OP.add))
            b.wait("scalar", "dv", vv)
            va = b.opf("scalar", "ac", lambda e: e.activation(
                out=statc[:, 48:56], in_=statc[:, 40:48], func=AF.Ln))
            b.wait("vector", "ac", va)
            vv = b.opf("vector", "dv", lambda e: e.tensor_scalar(
                out=statc[:, 48:56], in0=statc[:, 48:56], scalar1=-0.5,
                scalar2=None, op0=OP.mult))
            b.wait("scalar", "dv", vv)
            va = b.opf("scalar", "ac", lambda e: e.activation(
                out=statc[:, 56:64], in_=statc[:, 48:56], func=AF.Exp))
            b.wait("tensor", "dv", b.sems["dv"].n)
            b.wait("tensor", "ac", b.sems["ac"].n)
            for which, col0, row_t, ps in ((0, 16, mrow, pA), (1, 56, rrow, pB)):
                vp = 0
                for t in range(8):
                    if which == 1:
                        b.wait("tensor", "ac", va)
                    vp = b.op("tensor", "pe", lambda e, t=t, col0=col0: e.transpose(
                        pD[0:1, t * 128:(t + 1) * 128],
                        statc[:, col0 + t:col0 + t + 1], identf[:]))
                b.wait("vector", "pe", vp)
                vv = b.opf("vector", "dv", lambda e, row_t=row_t: e.tensor_copy(
                    out=row_t[:], in_=pD[0:1, 0:1024]))
                b.wait("tensor", "dv", vv)
                for hf in range(2):
                    vbc = b.op("tensor", "pe", lambda e, ps=ps, row_t=row_t, hf=hf:
                               e.matmul(ps[:, hf * 512:(hf + 1) * 512], onerf[:],
                                        row_t[0:1, hf * 512:(hf + 1) * 512],
                                        start=True, stop=True))
            v_yw = 0
            v_sub = [v_xb[0], v_xb[1]]
            b.wait("sync", "ac", max(v_xq))
            for g in range(NSLAB):
                bu = g % 2
                b.wait("sync", "dv", v_sub[bu])
                vld = b.dma("sync", "ld", xsl[:, bu * 1024:(bu + 1) * 1024],
                            src_dram[g * 128:(g + 1) * 128, :])
                b.wait("vector", "ld", vld)
                if g == 0:
                    b.wait("vector", "pe", vbc)
                v_sub[bu] = b.opf("vector", "dv", lambda e, bu=bu: e.tensor_sub(
                    out=ln_tmp[:], in0=xsl[:, bu * 1024:(bu + 1) * 1024],
                    in1=pA[:]))
                v_yw = b.opf("vector", "dv", lambda e, g=g: e.tensor_mul(
                    out=y[:, g * 1024:(g + 1) * 1024], in0=ln_tmp[:],
                    in1=pB[:]))
            return v_yw

        def emit_ffn(src_x):
            v_y_done = emit_ln(src_x, want_Y=True)
            if tap == "y":
                for g in range(NSLAB):
                    b.wait("gpsimd", "dv", v_y_done)
                    va = b.dma("gpsimd", "acc", xout[g * 128:(g + 1) * 128, :],
                               y[:, g * 1024:(g + 1) * 1024])
                    acc_total[0] = va
                return
            if tap == "pc":
                vv = b.op("vector", "dv", lambda e: e.tensor_copy(
                    out=qscr[:, 0:1024], in_=pC[:, 0:512]) if False else e.tensor_copy(
                    out=qscr[:, 0:512], in_=pC[:, 0:512]))
                vv = b.op("vector", "dv", lambda e: e.tensor_copy(
                    out=qscr[:, 512:1024], in_=pC[:, 512:1024]))
                b.wait("gpsimd", "dv", vv)
                va = b.dma("gpsimd", "acc", xout[256:384, :], qscr[:, 0:1024])
                va = b.dma("gpsimd", "acc", xout[384:512, :], xbfs[:, 0:1024])
                va = b.dma("gpsimd", "acc", xout[512:640, :], xsl[:, 0:1024])
                acc_total[0] = va
                return
            if tap == "stats":
                b.wait("gpsimd", "dv", v_y_done)
                va = b.dma("gpsimd", "acc", xout[0:32, :], Yy[:])
                va = b.dma("gpsimd", "acc", xout[128:256, 0:64], statc[:])
                acc_total[0] = va
                return
            v_deq = [0] * (NCH + 2)
            v_unp = [0] * (NCH + 2)
            v_mm = [0] * (NCH + 1)
            v_silu = [0] * (NPAIR + 1)
            v_hTT = [0] * (NPAIR + 1)
            v_qwld = [0] * (NCH + 2)
            v_sb_ok = [0] * (NCH + 2)
            v_srow = [0] * (NCH + 2)
            v_tt = [0] * (NCH + 2)
            v_sub_done = [0] * (NCH + 2)
            v_zrow = [0] * (NCH + 2)
            v_zb_ok = [0] * (NCH + 2)
            v_dnq = [[0] * 8 for _ in DN_KCS]
            dn_off = []
            _o = 0
            for _ns in DN_KCS:
                dn_off.append(_o)
                _o += _ns * 512
            v_dndeq = [[0] * 8 for _ in DN_KCS]
            v_dnsub = [[0] * 8 for _ in DN_KCS]
            v_dnzrow = [[0] * 8 for _ in DN_KCS]
            v_dnzb = [[0] * 8 for _ in DN_KCS]
            v_dnmm = [[0] * 64 for _ in DN_KCS]
            v_dnsb = [[0] * 8 for _ in DN_KCS]
            v_dnsrow = [[0] * 8 for _ in DN_KCS]


            def load_qw(c):
                if c > NCH - 1:
                    return
                b.wait("sync", "dv", v_tt[c - 2] if c >= 2 else 0)
                v_qwld[c] = b.dma("sync", "ld",
                                  qwg[:, (c % 2) * 512:(c % 2) * 512 + 512],
                                  ap["qwgu"][c])

            def load_srow(c):
                if c > NCH - 1:
                    return
                b.wait("gpsimd", "dbl", v_sb_ok[c - 1] - 7 * 16 if c >= 1 else 0)
                v_srow[c] = b.dma("gpsimd", "cst", srow[:], ap["sgu_rows"][c:c + 1, :])
                b.wait("gpsimd", "dbl", v_zb_ok[c - 1] - 7 * 16 if c >= 1 else 0)
                b.wait("gpsimd", "ld", v_ztab)
                v_zrow[c] = b.dma("gpsimd", "cst", srow2z[:],
                                  zgu_rows[c:c + 1, :])

            def bcast_s(c):
                if c > NCH - 1:
                    return
                b.wait("scalar", "dv", v_tt[c - 1] if c >= 1 else 0)
                b.wait("scalar", "cst", v_srow[c])
                v = b.dma("scalar", "dbl", sbg[0:1, :], srow[:])
                p = 1
                while p < 128:
                    b.wait("scalar", "dbl", v)
                    v = b.dma("scalar", "dbl", sbg[p:2 * p, :], sbg[0:p, :])
                    p *= 2
                v_sb_ok[c] = v
                b.wait("scalar", "dv", v_sub_done[c - 1] if c >= 1 else 0)
                b.wait("scalar", "cst", v_zrow[c])
                v = b.dma("scalar", "dbl", zbg[0:1, :], srow2z[:])
                p = 1
                while p < 128:
                    b.wait("scalar", "dbl", v)
                    v = b.dma("scalar", "dbl", zbg[p:2 * p, :], zbg[0:p, :])
                    p *= 2
                v_zb_ok[c] = v

            def dequant_gu(c):
                if c > NCH - 1:
                    return
                par = c % 2
                b.wait("vector", "ld", v_qwld[c])
                b.wait("vector", "dbl", v_sb_ok[c])
                b.wait("vector", "dbl", v_zb_ok[c])
                b.wait("vector", "pe", v_mm[c - 2] if c >= 2 else 0)
                qv = qwg[:, par * 512:(par + 1) * 512].bitcast(U8).rearrange(
                    "p (g bb) -> p g bb", bb=64)
                o = wi8[:].rearrange("p (g blk sub) -> p g blk sub", blk=2, sub=64)
                b.op("vector", "dv", lambda e: e.tensor_scalar(
                    out=o[:, :, 0, :], in0=qv, scalar1=15, scalar2=None,
                    op0=OP.bitwise_and))
                v_unp[c] = b.op("vector", "dv", lambda e: e.tensor_scalar(
                    out=o[:, :, 1, :], in0=qv, scalar1=4, scalar2=15,
                    op0=OP.logical_shift_right, op1=OP.bitwise_and))
                b.wait("vector", "dv", v_unp[c])
                v_sub_done[c] = b.op("vector", "dv", lambda e: e.tensor_sub(
                    out=wsub[:], in0=wi8[:], in1=zbg[:]))
                b.wait("vector", "dv", v_sub_done[c])
                v_tt[c] = b.opf("vector", "dv", lambda e, par=par: e.tensor_mul(
                    out=wdqg[:, par * 4096:(par + 1) * 4096], in0=wsub[:],
                    in1=sbg[:]))
                v_deq[c] = v_tt[c]

            def mm_chunk(c):
                par = c % 2
                i = c // 2
                is_gate = (c % 2 == 0)
                ps = pA if is_gate else pB
                b.wait("tensor", "dv", v_deq[c])
                if c <= 1:
                    b.wait("tensor", "dv", v_y_done)
                if is_gate:
                    b.wait("tensor", "ac", v_silu[i - 1] if i >= 1 else 0)
                else:
                    b.wait("tensor", "dv", v_hTT[i - 1] if i >= 1 else 0)
                for hf in range(2):
                    for k in range(NSLAB):
                        fn = (lambda e, k=k, hf=hf, par=par, ps=ps:
                              e.matmul(ps[:, hf * 512:(hf + 1) * 512],
                                       wdqg[:, par * 4096 + k * 128:
                                            par * 4096 + (k + 1) * 128],
                                       y[:, k * 1024 + hf * 512:
                                         k * 1024 + hf * 512 + 512],
                                       start=(k == 0), stop=(k == NSLAB - 1)))
                        if hf == 1 and k == NSLAB - 1:
                            v_mm[c] = b.op("tensor", "pe", fn)
                        else:
                            b.opn("tensor", fn)
                if is_gate:
                    b.wait("scalar", "pe", v_mm[c])
                    b.wait("scalar", "dv", v_hTT[i - 2] if i >= 2 else 0)
                    v_silu[i] = b.opf("scalar", "ac", lambda e, i=i: e.activation(
                        out=silu_sb[:, (i % 2) * 1024:(i % 2) * 1024 + 1024],
                        in_=pA[:], func=AF.Silu))
                else:
                    b.wait("vector", "pe", v_mm[c])
                    b.wait("vector", "ac", v_silu[i])
                    kc = i // 8
                    if kc >= 1:
                        b.wait("vector", "pe", v_dnmm[kc - 1][-1])
                    v_hTT[i] = b.opf("vector", "dv", lambda e, i=i: e.tensor_mul(
                        out=hbuf[:, (i % 8) * 1024:(i % 8) * 1024 + 1024],
                        in0=silu_sb[:, (i % 2) * 1024:(i % 2) * 1024 + 1024],
                        in1=pB[:]))

            def load_qwdn(kc, hcg):
                if kc > len(DN_KCS) - 1 or hcg > 7:
                    return
                ns = DN_KCS[kc]
                gi = kc * 8 + hcg
                slot = gi % 2
                if gi >= 2:
                    pk, ph = divmod(gi - 2, 8)
                    b.wait("sync", "dv", v_dndeq[pk][ph])
                off = dn_off[kc] + hcg * ns * 64
                v_dnq[kc][hcg] = b.dma("sync", "ld", qwd[:, slot * 512:slot * 512 + ns * 64],
                                       ap["qwdn"][:, off:off + ns * 64])

            def dn_srow(kc, hcg):
                b.wait("gpsimd", "dbl",
                       v_dnsb[kc][hcg - 1] - 7 * 16 if hcg >= 1 else
                       (v_dnsb[kc - 1][7] - 7 * 16 if kc >= 1 else 0))
                v_dnsrow[kc][hcg] = b.dma("gpsimd", "cst", srow2[:],
                                          ap["sdn_rows"][kc * 8 + hcg:kc * 8 + hcg + 1, :])
                b.wait("gpsimd", "dbl",
                       v_dnzb[kc][hcg - 1] - 7 * 16 if hcg >= 1 else
                       (v_dnzb[kc - 1][7] - 7 * 16 if kc >= 1 else 0))
                b.wait("gpsimd", "ld", v_ztab)
                v_dnzrow[kc][hcg] = b.dma("gpsimd", "cst", srowdz[:],
                                          zdn_rows[kc * 8 + hcg:kc * 8 + hcg + 1, :])

            def dn_bcast(kc, hcg):
                ns = DN_KCS[kc]
                par = hcg % 2
                if hcg >= 2:
                    b.wait("scalar", "dv", v_dndeq[kc][hcg - 2])
                elif kc >= 1:
                    b.wait("scalar", "dv", v_dndeq[kc - 1][6 + hcg])
                b.wait("scalar", "cst", v_dnsrow[kc][hcg])
                v = b.dma("scalar", "dbl", sbd[0:1, par * 4096:par * 4096 + ns * 512],
                          srow2[0:1, 0:ns * 512])
                p = 1
                while p < 128:
                    b.wait("scalar", "dbl", v)
                    v = b.dma("scalar", "dbl",
                              sbd[p:2 * p, par * 4096:par * 4096 + ns * 512],
                              sbd[0:p, par * 4096:par * 4096 + ns * 512])
                    p *= 2
                v_dnsb[kc][hcg] = v
                if hcg >= 1:
                    b.wait("scalar", "dv", v_dnsub[kc][hcg - 1])
                elif kc >= 1:
                    b.wait("scalar", "dv", v_dnsub[kc - 1][7])
                b.wait("scalar", "cst", v_dnzrow[kc][hcg])
                ns512 = DN_KCS[kc] * 512
                v = b.dma("scalar", "dbl", zbd[0:1, 0:ns512], srowdz[0:1, 0:ns512])
                p = 1
                while p < 128:
                    b.wait("scalar", "dbl", v)
                    v = b.dma("scalar", "dbl", zbd[p:2 * p, 0:ns512],
                              zbd[0:p, 0:ns512])
                    p *= 2
                v_dnzb[kc][hcg] = v

            def dn_dequant(kc, hcg):
                ns = DN_KCS[kc]
                par = hcg % 2
                b.wait("vector", "ld", v_dnq[kc][hcg])
                b.wait("vector", "dbl", v_dnsb[kc][hcg])
                b.wait("vector", "dbl", v_dnzb[kc][hcg])
                if hcg >= 2:
                    b.wait("vector", "pe", v_dnmm[kc][(hcg - 1) * 8 - 1])
                elif kc >= 1:
                    b.wait("vector", "pe", v_dnmm[kc - 1][(7 + hcg - 1) * 8 - 1])
                slot = (kc * 8 + hcg) % 2
                qv = qwd[:, slot * 512:slot * 512 + ns * 64].bitcast(U8).rearrange(
                    "p (g hc bb) -> p g hc bb", hc=4, bb=64)
                o = wi8[:, 0:ns * 512].rearrange(
                    "p (g hc blk sub) -> p g hc blk sub", hc=4, blk=2, sub=64)
                b.op("vector", "dv", lambda e: e.tensor_scalar(
                    out=o[:, :, :, 0, :], in0=qv, scalar1=15, scalar2=None,
                    op0=OP.bitwise_and))
                vv = b.op("vector", "dv", lambda e: e.tensor_scalar(
                    out=o[:, :, :, 1, :], in0=qv, scalar1=4, scalar2=15,
                    op0=OP.logical_shift_right, op1=OP.bitwise_and))
                b.wait("vector", "dv", vv)
                v_dnsub[kc][hcg] = b.op("vector", "dv",
                                        lambda e, ns=ns: e.tensor_sub(
                                            out=wsub[:, 0:ns * 512],
                                            in0=wi8[:, 0:ns * 512],
                                            in1=zbd[:, 0:ns * 512]))
                b.wait("vector", "dv", v_dnsub[kc][hcg])
                v_dndeq[kc][hcg] = b.opf("vector", "dv",
                                         lambda e, par=par, ns=ns: e.tensor_mul(
                                             out=wdqd[:, par * 4096:par * 4096 + ns * 512],
                                             in0=wsub[:, 0:ns * 512],
                                             in1=sbd[:, par * 4096:par * 4096 + ns * 512]))

            def dn_phase2(kc):
                ns = DN_KCS[kc]
                last = False
                if tap == "dn3" and kc == 3:
                    b.wait("gpsimd", "dv", v_hTT[31])
                    for sl in range(8):
                        va = b.dma("gpsimd", "acc", xout[sl * 128:(sl + 1) * 128, :],
                                   hbuf[:, sl * 1024:(sl + 1) * 1024])
                    acc_total[0] = va
                    tap_stop[0] = True
                    return
                if tap == "dn0" and kc == 0:
                    b.wait("gpsimd", "dv", v_hTT[7])
                    for sl in range(8):
                        va = b.dma("gpsimd", "acc", xout[sl * 128:(sl + 1) * 128, :],
                                   hbuf[:, sl * 1024:(sl + 1) * 1024])
                    acc_total[0] = va
                    tap_stop[0] = True
                    return
                for ht in range(32):
                    hcg = ht // 4
                    hc = ht % 4
                    par = hcg % 2
                    if hc == 0:
                        dn_srow(kc, hcg)
                        dn_bcast(kc, hcg)
                        dn_dequant(kc, hcg)
                        gi = kc * 8 + hcg + 2
                        load_qwdn(gi // 8, gi % 8)
                    b.wait("tensor", "dv", v_dndeq[kc][hcg])
                    for hf in range(2):
                        piece = ht * 2 + hf
                        slot = piece % 2
                        b.wait("tensor", "ac", evac_slot[slot])
                        for g in range(ns):
                            fn = (lambda e, g=g, hc=hc, hf=hf, par=par, slot=slot,
                                  stop=(g == ns - 1):
                                  e.matmul(pC[:, slot * 512:slot * 512 + 512],
                                           wdqd[:, par * 4096 + g * 512 + hc * 128:
                                                par * 4096 + g * 512 + (hc + 1) * 128],
                                           hbuf[:, g * 1024 + hf * 512:
                                                g * 1024 + hf * 512 + 512],
                                           start=(g == 0), stop=stop))
                            if g == ns - 1:
                                v_dnmm[kc][piece] = b.op("tensor", "pe", fn)
                            else:
                                b.opn("tensor", fn)
                        if tap == "dnp" and kc == 0 and ht == 0 and hf == 1:
                            b.wait("vector", "pe", v_dnmm[0][1])
                            vv = b.opf("vector", "dv", lambda e: e.tensor_copy(
                                out=ln_tmp[:], in_=pC[:]))
                            b.wait("gpsimd", "dv", vv)
                            va = b.dma("gpsimd", "acc", xout[0:128, :], ln_tmp[:])
                            b.wait("vector", "acc", va)
                            for sl in range(8):
                                va = b.dma("gpsimd", "acc",
                                           xout[(1 + sl) * 128:(2 + sl) * 128, 0:512],
                                           wdqd[:, sl * 512:sl * 512 + 512])
                            for q in range(4):
                                vv = b.opf("vector", "dv", lambda e, q=q: e.tensor_copy(
                                    out=ln_tmp[:], in_=wi8[:, q * 1024:(q + 1) * 1024]))
                                b.wait("gpsimd", "dv", vv)
                                va = b.dma("gpsimd", "acc",
                                           xout[(9 + q) * 128:(10 + q) * 128, :],
                                           ln_tmp[:])
                                b.wait("vector", "acc", va)
                                vv = b.opf("vector", "dv", lambda e, q=q: e.tensor_copy(
                                    out=ln_tmp[:], in_=sbd[:, q * 1024:(q + 1) * 1024]))
                                b.wait("gpsimd", "dv", vv)
                                va = b.dma("gpsimd", "acc",
                                           xout[(13 + q) * 128:(14 + q) * 128, :],
                                           ln_tmp[:])
                                b.wait("vector", "acc", va)
                            acc_total[0] = va
                            tap_stop[0] = True
                            return
                        b.wait("scalar", "pe", v_dnmm[kc][piece])
                        stg = ht % 2
                        if hf == 0:
                            b.wait("scalar", "acc", dnst_free[stg])
                        b.opn("scalar", lambda e, stg=stg, hf=hf, slot=slot:
                              e.activation(out=dnst[:, stg * 1024 + hf * 512:
                                                    stg * 1024 + hf * 512 + 512],
                                           in_=pC[:, slot * 512:slot * 512 + 512],
                                           func=AF.Copy, scale=0.5))
                        vev = b.fence("scalar", "ac")
                        evac_slot[slot] = vev
                        if hf == 1:
                            b.wait("gpsimd", "ac", vev)
                            b.wait("gpsimd", "ld", v_d2d)
                            b.wait("gpsimd", "acc", v_acc_region.get(ht, 0))
                            va = b.dma("gpsimd", "acc",
                                       xout[ht * 128:(ht + 1) * 128, :],
                                       dnst[:, stg * 1024:stg * 1024 + 1024],
                                       accum_op=OP.add)
                            v_acc_region[ht] = va
                            dnst_free[stg] = va
                            acc_total[0] = va

            tap_stop = [False]
            load_qw(0)
            load_qw(1)
            load_srow(0)
            bcast_s(0)
            load_qwdn(0, 0)
            load_qwdn(0, 1)
            dequant_gu(0)
            if tap == "dq0":
                b.wait("vector", "dv", v_tt[0])
                for nm, src_ap, rows in (
                        ("wi8", wi8[:].bitcast(U8), 4),
                        ("zbg", zbg, 4),
                        ("wsub", wsub, 4),
                        ("wdqg", wdqg[:, 0:4096], 4),
                        ("sbg", sbg, 4)):
                    pass
                va = 0
                for q in range(4):
                    vv = b.opf("vector", "dv", lambda e, q=q: e.tensor_copy(
                        out=ln_tmp[:], in_=wi8[:, q * 1024:(q + 1) * 1024]))
                    b.wait("gpsimd", "dv", vv)
                    va = b.dma("gpsimd", "acc", xout[q * 128:(q + 1) * 128, :], ln_tmp[:])
                    b.wait("vector", "acc", va)
                    vv = b.opf("vector", "dv", lambda e, q=q: e.tensor_copy(
                        out=ln_tmp[:], in_=zbg[:, q * 1024:(q + 1) * 1024]))
                    b.wait("gpsimd", "dv", vv)
                    va = b.dma("gpsimd", "acc", xout[(4 + q) * 128:(5 + q) * 128, :], ln_tmp[:])
                    b.wait("vector", "acc", va)
                    vv = b.opf("vector", "dv", lambda e, q=q: e.tensor_copy(
                        out=ln_tmp[:], in_=wsub[:, q * 1024:(q + 1) * 1024]))
                    b.wait("gpsimd", "dv", vv)
                    va = b.dma("gpsimd", "acc", xout[(8 + q) * 128:(9 + q) * 128, :], ln_tmp[:])
                    b.wait("vector", "acc", va)
                    vv = b.opf("vector", "dv", lambda e, q=q: e.tensor_copy(
                        out=ln_tmp[:], in_=wdqg[:, q * 1024:(q + 1) * 1024]))
                    b.wait("gpsimd", "dv", vv)
                    va = b.dma("gpsimd", "acc", xout[(12 + q) * 128:(13 + q) * 128, :], ln_tmp[:])
                    b.wait("vector", "acc", va)
                    vv = b.opf("vector", "dv", lambda e, q=q: e.tensor_copy(
                        out=ln_tmp[:], in_=sbg[:, q * 1024:(q + 1) * 1024]))
                    b.wait("gpsimd", "dv", vv)
                    va = b.dma("gpsimd", "acc", xout[(16 + q) * 128:(17 + q) * 128, :], ln_tmp[:])
                    b.wait("vector", "acc", va)
                for r in range(4):
                    va = b.dma("gpsimd", "acc", xout[2560 + r:2561 + r, 0:1024],
                               zgu_rows[r:r + 1, 0:1024])
                acc_total[0] = va
                return
            load_srow(1)
            bcast_s(1)
            dequant_gu(1)
            for c in range(NCH):
                load_qw(c + 2)
                load_srow(c + 2)
                bcast_s(c + 2)
                mm_chunk(c)
                if tap in ("gu_a", "gu_b") and c == 0:
                    b.wait("vector", "pe", v_mm[0])
                    vv = b.opf("vector", "dv", lambda e: e.tensor_copy(
                        out=ln_tmp[:], in_=pA[:]))
                    b.wait("gpsimd", "dv", vv)
                    va = b.dma("gpsimd", "acc", xout[0:128, :], ln_tmp[:])
                    b.wait("vector", "acc", va)
                    for q in range(4):
                        vv = b.opf("vector", "dv", lambda e, q=q: e.tensor_copy(
                            out=ln_tmp[:], in_=wdqg[:, q * 1024:(q + 1) * 1024]))
                        b.wait("gpsimd", "dv", vv)
                        va = b.dma("gpsimd", "acc",
                                   xout[(1 + q) * 128:(2 + q) * 128, :], ln_tmp[:])
                        b.wait("vector", "acc", va)
                        vv = b.opf("vector", "dv", lambda e, q=q: e.tensor_copy(
                            out=ln_tmp[:], in_=zbg[:, q * 1024:(q + 1) * 1024]))
                        b.wait("gpsimd", "dv", vv)
                        va = b.dma("gpsimd", "acc",
                                   xout[(5 + q) * 128:(6 + q) * 128, :], ln_tmp[:])
                        b.wait("vector", "acc", va)
                    acc_total[0] = va
                    return
                dequant_gu(c + 2)
                if tap == "dq2" and c == 0:
                    b.wait("vector", "dv", v_tt[2])
                    va = 0
                    for q in range(4):
                        vv = b.opf("vector", "dv", lambda e, q=q: e.tensor_copy(
                            out=ln_tmp[:], in_=wi8[:, q * 1024:(q + 1) * 1024]))
                        b.wait("gpsimd", "dv", vv)
                        va = b.dma("gpsimd", "acc", xout[q * 128:(q + 1) * 128, :], ln_tmp[:])
                        b.wait("vector", "acc", va)
                        vv = b.opf("vector", "dv", lambda e, q=q: e.tensor_copy(
                            out=ln_tmp[:], in_=zbg[:, q * 1024:(q + 1) * 1024]))
                        b.wait("gpsimd", "dv", vv)
                        va = b.dma("gpsimd", "acc", xout[(4 + q) * 128:(5 + q) * 128, :], ln_tmp[:])
                        b.wait("vector", "acc", va)
                        vv = b.opf("vector", "dv", lambda e, q=q: e.tensor_copy(
                            out=ln_tmp[:], in_=wdqg[:, q * 1024:(q + 1) * 1024]))
                        b.wait("gpsimd", "dv", vv)
                        va = b.dma("gpsimd", "acc", xout[(8 + q) * 128:(9 + q) * 128, :], ln_tmp[:])
                        b.wait("vector", "acc", va)
                        vv = b.opf("vector", "dv", lambda e, q=q: e.tensor_copy(
                            out=ln_tmp[:], in_=sbg[:, q * 1024:(q + 1) * 1024]))
                        b.wait("gpsimd", "dv", vv)
                        va = b.dma("gpsimd", "acc", xout[(12 + q) * 128:(13 + q) * 128, :], ln_tmp[:])
                        b.wait("vector", "acc", va)
                    acc_total[0] = va
                    tap_stop[0] = True
                    return
                if tap == "gu0" and c == 1:
                    vv = b.opf("vector", "dv", lambda e: e.tensor_copy(
                        out=qscr[:, 0:1024], in_=pB[:]))
                    vv = b.opf("vector", "dv", lambda e: e.tensor_copy(
                        out=ln_tmp[:], in_=pA[:]))
                    b.wait("gpsimd", "dv", max(vv, v_hTT[0]))
                    b.wait("gpsimd", "ac", v_silu[0])
                    va = b.dma("gpsimd", "acc", xout[0 * 128:1 * 128, :],
                               wdqg[:, 0:1024])
                    va = b.dma("gpsimd", "acc", xout[1 * 128:2 * 128, :],
                               hbuf[:, 0:1024])
                    va = b.dma("gpsimd", "acc", xout[2 * 128:3 * 128, :],
                               qscr[:, 0:1024])
                    va = b.dma("gpsimd", "acc", xout[3 * 128:4 * 128, :],
                               silu_sb[:, 0:1024])
                    va = b.dma("gpsimd", "acc", xout[4 * 128:5 * 128, :],
                               ln_tmp[:])
                    va = b.dma("gpsimd", "acc", xout[5 * 128:6 * 128, :],
                               y[:, 0:1024])
                    va = b.dma("gpsimd", "acc", xout[6 * 128:6 * 128 + 32, :],
                               ngc[0:32, 0:1024])
                    va = b.dma("gpsimd", "acc", xout[7 * 128:7 * 128 + 32, :],
                               Yy[:])
                    for q in range(4):
                        va = b.dma("gpsimd", "acc",
                                   xout[(8 + q) * 128:(9 + q) * 128, :],
                                   wdqg[:, q * 1024:(q + 1) * 1024])
                    acc_total[0] = va
                    return
                i = c // 2
                if c % 2 == 1 and ((i + 1) % 8 == 0 or i == NPAIR - 1):
                    kc = i // 8
                    dn_phase2(kc)
                    if tap in ("dnacc", "dnacc3") and kc == (0 if tap == "dnacc" else 3):
                        return
                    if tap_stop[0]:
                        return
            return

        def emit_attn():
            v_y2 = emit_ln(xout, want_Y=False)
            v_wq = [0] * 10
            v_qmm = [0] * 9
            v_bias = [0] * 9
            v_sq = [0] * 9
            v_den = [0] * 9
            for h in range(8):
                par = h % 2
                if h >= 2:
                    b.wait("gpsimd", "pe", v_qmm[h - 2])
                v_wq[h] = b.dma("gpsimd", "cst",
                                wtile[:, par * 4096:(par + 1) * 4096], ap["wq"][h])
                b.wait("tensor", "cst", v_wq[h])
                if h == 0:
                    b.wait("tensor", "dv", v_y2)
                if h >= 1:
                    b.wait("tensor", "dv", v_bias[h - 1])
                for hf in range(2):
                    for k in range(NSLAB):
                        fn = (lambda e, k=k, hf=hf, par=par: e.matmul(
                            pA[:, hf * 512:(hf + 1) * 512],
                            wtile[:, par * 4096 + k * 128:par * 4096 + (k + 1) * 128],
                            y[:, k * 1024 + hf * 512:k * 1024 + hf * 512 + 512],
                            start=(k == 0), stop=(k == NSLAB - 1)))
                        if hf == 1 and k == NSLAB - 1:
                            v_qmm[h] = b.op("tensor", "pe", fn)
                        else:
                            b.opn("tensor", fn)
                b.wait("vector", "pe", v_qmm[h])
                v_bias[h] = b.opf("vector", "dv", lambda e, h=h: e.tensor_scalar(
                    out=qscr[:, 0:1024], in0=pA[:], scalar1=bq8[:, h:h + 1],
                    scalar2=None, op0=OP.add))
                b.wait("vector", "dv", v_bias[h])
                vv = b.opf("vector", "dv", lambda e: e.tensor_scalar(
                    out=qscr[:, 1024:2048], in0=qscr[:, 0:1024], scalar1=0.0,
                    scalar2=None, op0=OP.min))
                b.wait("scalar", "dv", vv)
                va = b.opf("scalar", "ac", lambda e: e.activation(
                    out=silu_sb[:, 0:1024].bitcast(F32)[:, 0:512] if False else
                    silu_sb[:].bitcast(F32)[:, 0:1024],
                    in_=qscr[:, 1024:2048], func=AF.Exp))
                b.wait("vector", "ac", va)
                vv = b.op("vector", "dv", lambda e: e.tensor_scalar(
                    out=qscr[:, 1024:2048], in0=qscr[:, 0:1024], scalar1=0.0,
                    scalar2=None, op0=OP.max))
                b.wait("vector", "dv", vv)
                v_sq[h] = b.opf("vector", "dv", lambda e, h=h: e.tensor_add(
                    out=sq[:, h * 1024:(h + 1) * 1024],
                    in0=silu_sb[:].bitcast(F32)[:, 0:1024],
                    in1=qscr[:, 1024:2048]))
                b.wait("tensor", "dv", v_sq[h])
                for t in range(8):
                    fn = (lambda e, t=t, h=h: e.matmul(
                        pD[:, t * 8 + h:t * 8 + h + 1],
                        sq[:, h * 1024 + t * 128:h * 1024 + (t + 1) * 128],
                        z8[:, h:h + 1], start=True, stop=True))
                    if t == 7:
                        v_den[h] = b.op("tensor", "pe", fn)
                    else:
                        b.opn("tensor", fn)
            b.wait("vector", "pe", v_den[7])
            vv = b.op("vector", "dv", lambda e: e.tensor_scalar(
                out=rcp[:, 0:64], in0=pD[:, 0:64], scalar1=1e-6, scalar2=None,
                op0=OP.add))
            b.wait("vector", "dv", vv)
            v_rcp = b.opf("vector", "dv", lambda e: e.reciprocal(
                out=rcp[:, 64:128], in_=rcp[:, 0:64]))
            v_nummm = [0] * 9
            v_retrev = [0] * 9
            v_sqn = [0] * 9
            for h in range(8):
                b.wait("tensor", "dv", v_rcp)
                vp = 0
                for t in range(8):
                    vp = b.op("tensor", "pe", lambda e, t=t, h=h: e.transpose(
                        pD[0:1, t * 128:(t + 1) * 128],
                        rcp[:, 64 + t * 8 + h:64 + t * 8 + h + 1], identf[:]))
                b.wait("vector", "pe", vp)
                vv = b.opf("vector", "dv", lambda e: e.tensor_copy(
                    out=rrow[:], in_=pD[0:1, 0:1024]))
                b.wait("tensor", "dv", vv)
                vb = 0
                for hf in range(2):
                    vb = b.op("tensor", "pe", lambda e, hf=hf: e.matmul(
                        pB[:, hf * 512:(hf + 1) * 512], onerf[:],
                        rrow[0:1, hf * 512:(hf + 1) * 512], start=True, stop=True))
                b.wait("vector", "pe", vb)
                if h >= 2:
                    b.wait("vector", "pe", v_nummm[h - 2])
                v_sqn[h] = b.opf("vector", "dv", lambda e, h=h: e.tensor_mul(
                    out=silu_sb[:, (h % 2) * 1024:(h % 2) * 1024 + 1024],
                    in0=sq[:, h * 1024:(h + 1) * 1024], in1=pB[:]))
                b.wait("tensor", "dv", v_sqn[h])
                b.wait("tensor", "cst", v_m8)
                if h >= 1:
                    b.wait("tensor", "dv", v_retrev[h - 1])
                for hf in range(2):
                    fn = (lambda e, hf=hf, h=h: e.matmul(
                        pA[:, hf * 512:(hf + 1) * 512],
                        m8[:, h * 128:(h + 1) * 128],
                        silu_sb[:, (h % 2) * 1024 + hf * 512:
                                (h % 2) * 1024 + hf * 512 + 512],
                        start=True, stop=True))
                    if hf == 1:
                        v_nummm[h] = b.op("tensor", "pe", fn)
                    else:
                        b.opn("tensor", fn)
                b.wait("vector", "pe", v_nummm[h])
                v_retrev[h] = b.opf("vector", "dv", lambda e, h=h: e.tensor_copy(
                    out=retr[:, h * 1024:(h + 1) * 1024], in_=pA[:]))
            v_retr = b.sems["dv"].n
            v_wo = [0] * 34
            v_att = [0] * 66

            for nt in range(32):
                par = nt % 2
                if nt >= 2:
                    b.wait("gpsimd", "pe", v_att[(nt - 2) * 2 + 1])
                v_wo[nt] = b.dma("gpsimd", "cst",
                                 wtile[:, par * 4096:(par + 1) * 4096], ap["wo"][nt])
                b.wait("tensor", "cst", v_wo[nt])
                b.wait("tensor", "dv", v_retr)
                for hf in range(2):
                    slot = (nt * 2 + hf) % 2
                    b.wait("tensor", "ac", evac_slot[slot])
                    for k in range(32):
                        fn = (lambda e, k=k, hf=hf, par=par, slot=slot: e.matmul(
                            pC[:, slot * 512:slot * 512 + 512],
                            wtile[:, par * 4096 + k * 128:par * 4096 + (k + 1) * 128],
                            retr[:, (k // 4) * 1024 + hf * 512:
                                 (k // 4) * 1024 + hf * 512 + 512],
                            start=(k == 0), stop=(k == 31)))
                        if k == 31:
                            v_att[nt * 2 + hf] = b.op("tensor", "pe", fn)
                        else:
                            b.opn("tensor", fn)
                    b.wait("scalar", "pe", v_att[nt * 2 + hf])
                    stg = nt % 2
                    if hf == 0:
                        b.wait("scalar", "acc", dnst_free[stg])
                    b.opn("scalar", lambda e, stg=stg, hf=hf, slot=slot:
                          e.activation(out=dnst[:, stg * 1024 + hf * 512:
                                                stg * 1024 + hf * 512 + 512],
                                       in_=pC[:, slot * 512:slot * 512 + 512],
                                       func=AF.Copy, scale=1.0))
                    vev = b.fence("scalar", "ac")
                    evac_slot[slot] = vev
                    if hf == 1:
                        b.wait("gpsimd", "ac", vev)
                        b.wait("gpsimd", "acc", v_acc_region.get(nt, 0))
                        va = b.dma("gpsimd", "acc", xout[nt * 128:(nt + 1) * 128, :],
                                   dnst[:, stg * 1024:stg * 1024 + 1024],
                                   accum_op=OP.add)
                        v_acc_region[nt] = va
                        dnst_free[stg] = va
                        acc_total[0] = va

        b.wait("vector", "dv", v_init_done)
        if n_sub >= 1:
            emit_ffn(ap["xin"])
        if n_sub >= 2:
            emit_attn()
        if n_sub >= 3:
            emit_ffn(xout)
        b.wait("gpsimd", "acc", acc_total[0])
        if n_sub == 0:
            b.wait("gpsimd", "ld", v_d2d)

        with nc.Block() as block:
            @block.sync
            def _(sync):
                for fn in b.q["sync"]:
                    fn(sync)

            @block.gpsimd
            def _(gp):
                for fn in b.q["gpsimd"]:
                    fn(gp)

            @block.scalar
            def _(sc):
                for fn in b.q["scalar"]:
                    fn(sc)

            @block.vector
            def _(ve):
                for fn in b.q["vector"]:
                    fn(ve)

            @block.tensor
            def _(te):
                for fn in b.q["tensor"]:
                    fn(te)

    return nc


def kernel(**inputs):
    w = stage_weights(inputs)
    x = np.asarray(inputs["L0_attn_hidden_states"], dtype=np.float32)
    x2 = x.reshape(NCORES * TPC, HID)[:, PERMH]
    xT = np.ascontiguousarray(x2.T)

    nc = build_nc(N_SUBLAYERS)
    in_maps = []
    for c in range(NCORES):
        m = dict(w)
        m["xin"] = np.ascontiguousarray(xT[:, c * TPC:(c + 1) * TPC])
        in_maps.append(m)
    res = run_bass_kernel_spmd(nc, in_maps, core_ids=list(range(NCORES)))
    out = np.concatenate([r["xout"] for r in res.results], axis=1)
    full = np.empty((NCORES * TPC, HID), dtype=np.float32)
    full[:, PERMH] = out.T
    return full.reshape(2, 4096, HID)


# version 06
# speedup vs baseline: 2.6535x; 2.6535x over previous
import numpy as np
import ml_dtypes
from contextlib import ExitStack

import concourse.bass as bass
import concourse.mybir as mybir
from concourse.bass_utils import run_bass_kernel_spmd

F32 = mybir.dt.float32
BF16 = mybir.dt.bfloat16
I32 = mybir.dt.int32
U8 = mybir.dt.uint8
I8 = mybir.dt.int8
OP = mybir.AluOpType
AF = mybir.ActivationFunctionType
AX = mybir.AxisListType
bf16 = ml_dtypes.bfloat16

HID = 4096
INTER = 11008
NCORES = 8
TPC = 1024
NSLAB = 32
NPAIR = 86
NCH = 172
DN_KCS = [8] * 10 + [6]
NPIECE = 22
N_SUBLAYERS = 3

_p = np.arange(64)
PI128 = np.concatenate([8 * (_p // 4) + 2 * (_p % 4),
                        8 * (_p // 4) + 2 * (_p % 4) + 1])


def perm(n):
    i = np.arange(n)
    return (i // 128) * 128 + PI128[i % 128]


PERMH = perm(HID)
PERMI = perm(INTER)
PERMQ = perm(1024)


def stage_weights(inp):
    d = {}
    f32 = np.float32

    A = np.asarray(inp["L0_ffn_qweight_gu"])[PERMH]
    gate_w = A[:, :INTER // 8].reshape(HID, NPAIR, 16)
    up_w = A[:, INTER // 8:].reshape(HID, NPAIR, 16)
    ch = np.empty((NCH, HID, 16), dtype=np.int32)
    ch[0::2] = gate_w.transpose(1, 0, 2)
    ch[1::2] = up_w.transpose(1, 0, 2)
    d["qwgu"] = np.ascontiguousarray(
        ch.reshape(NCH, NSLAB, 128, 16).transpose(0, 2, 1, 3)).reshape(NCH, 128, 512)

    S = np.asarray(inp["L0_ffn_scales_gu"]).astype(f32)
    gate_s = S[:, :INTER].reshape(32, NPAIR, 128)[:, :, PI128]
    up_s = S[:, INTER:].reshape(32, NPAIR, 128)[:, :, PI128]
    srows = np.empty((NCH, 32, 128), dtype=f32)
    srows[0::2] = gate_s.transpose(1, 0, 2)
    srows[1::2] = up_s.transpose(1, 0, 2)
    d["sgu_rows"] = np.ascontiguousarray(srows.reshape(NCH, 4096))
    d["sgu_exp"] = np.broadcast_to(
        srows.reshape(NCH, 1, 4096), (NCH, 128, 4096)).astype(ml_dtypes.bfloat16)
    d["sgu_g"] = np.ascontiguousarray(
        srows.transpose(1, 0, 2).reshape(32, NCH * 128))
    d["qzgu"] = np.ascontiguousarray(np.asarray(inp["L0_ffn_qzeros_gu"]))

    Bm = np.asarray(inp["L0_ffn_qweight_dn"])[PERMI]
    Bk = Bm.reshape(NPAIR, 128, 32, 16)
    blocks = []
    s0 = 0
    for ns in DN_KCS:
        Ak = Bk[s0:s0 + ns]
        for hcg in range(8):
            blk = Ak[:, :, 4 * hcg:4 * hcg + 4, :]
            blocks.append(blk.transpose(1, 0, 2, 3).reshape(128, ns * 64))
        s0 += ns
    d["qwdn"] = np.ascontiguousarray(np.concatenate(blocks, axis=1))

    Sd = np.asarray(inp["L0_ffn_scales_dn"]).astype(f32)
    Sd_int = Sd.reshape(NPAIR, 32, 128)[:, :, PI128]
    rows = np.zeros((len(DN_KCS) * 8, 4096), dtype=f32)
    s0 = 0
    for kc, ns in enumerate(DN_KCS):
        for hcg in range(8):
            blk = Sd_int[s0:s0 + ns, 4 * hcg:4 * hcg + 4, :]
            rows[kc * 8 + hcg, :ns * 512] = blk.reshape(-1)
        s0 += ns
    d["sdn_rows"] = rows
    d["sdn_exp"] = np.broadcast_to(
        rows.reshape(len(DN_KCS) * 8, 1, 4096),
        (len(DN_KCS) * 8, 128, 4096)).astype(ml_dtypes.bfloat16)
    d["sdn_g"] = np.ascontiguousarray(Sd_int.reshape(NPAIR, 4096))
    d["qzdn"] = np.ascontiguousarray(np.asarray(inp["L0_ffn_qzeros_dn"]))

    Wq = np.asarray(inp["L0_attn_Wq"]).astype(f32)
    sel = np.concatenate([np.arange(4 * h * 128, 4 * h * 128 + 128)
                          for h in range(8)])
    wqT = Wq[sel][PERMQ][:, PERMH].T
    d["wq"] = np.ascontiguousarray(
        wqT.reshape(NSLAB, 128, 8, 128).transpose(2, 1, 0, 3)).reshape(8, 128, 4096)
    bq = np.asarray(inp["L0_attn_bq"]).astype(f32)[sel][PERMQ]
    d["bq8"] = np.ascontiguousarray(bq.reshape(8, 128).T)
    M = np.asarray(inp["L0_attn_M"]).astype(f32)
    d["m8"] = np.ascontiguousarray(M[:, PI128][:, :, PI128])
    z = np.asarray(inp["L0_attn_z"]).astype(f32)
    d["z8"] = np.ascontiguousarray(z[:, PI128].T)
    Wo = np.asarray(inp["L0_attn_Wo"]).astype(f32)
    woT = Wo.T[perm(4096)][:, PERMH]
    d["wo"] = np.ascontiguousarray(
        woT.reshape(32, 128, 32, 128).transpose(2, 1, 0, 3)).reshape(32, 128, 4096)

    d["ones_col"] = np.ones((128, 1), dtype=f32)
    d["ones_row"] = np.ones((1, 128), dtype=f32)
    d["ident"] = np.eye(128, dtype=f32)
    return d


class Sem:
    def __init__(self, h):
        self.h = h
        self.n = 0


class B:

    def __init__(self, nc, stack):
        self.nc = nc
        self.stack = stack
        self.q = {e: [] for e in ("sync", "gpsimd", "scalar", "vector", "tensor")}
        self.sems = {}
        for s in ("ld", "cst", "dbl", "acc", "ac0", "ac1", "ac2", "ac3",
                  "dv", "pe", "ac"):
            self.sems[s] = Sem(stack.enter_context(nc.semaphore("s_" + s)))

    def sb(self, name, p, cols, dt):
        return self.stack.enter_context(self.nc.sbuf_tensor(name, [p, cols], dt))

    def dma(self, eng, sem, out, in_, **kw):
        s = self.sems[sem]
        s.n += 16
        self.q[eng].append(lambda e, out=out, in_=in_, h=s.h, kw=kw:
                           e.dma_start(out=out, in_=in_, **kw).then_inc(h, 16))
        return s.n

    def wait(self, eng, sem, v):
        if v <= 0:
            return
        if sem in ("ld", "cst", "dbl", "acc", "ac0", "ac1", "ac2", "ac3"):
            v = self.sems[sem].n
        h = self.sems[sem].h
        self.q[eng].append(lambda e, h=h, v=v: e.wait_ge(h, v))

    def op(self, eng, sem, fn):
        s = self.sems[sem]
        s.n += 1
        self.q[eng].append(lambda e, fn=fn, h=s.h: fn(e).then_inc(h, 1))
        return s.n

    def opn(self, eng, fn):
        self.q[eng].append(fn)

    def fence(self, eng, sem):
        s = self.sems[sem]
        s.n += 1
        if eng == "vector":
            self.q[eng].append(lambda e, h=s.h: e.tensor_copy(
                out=self.fbuf[0:1, 0:2], in_=self.fbuf[0:1, 2:4]).then_inc(h, 1))
        elif eng == "scalar":
            self.q[eng].append(lambda e, h=s.h: e.copy(
                out=self.fbuf[0:1, 4:6], in_=self.fbuf[0:1, 6:8]).then_inc(h, 1))
        else:
            raise ValueError(eng)
        return s.n

    def opf(self, eng, sem, fn):
        self.opn(eng, lambda e, fn=fn: fn(e))
        return self.fence(eng, sem)

    def barrier(self, eng, sems):
        for s in sems:
            self.wait(eng, s, self.sems[s].n)


def build_nc(n_sub=N_SUBLAYERS, tap=None):
    nc = bass.Bass("TRN2", target_bir_lowering=False)
    ap = {}
    for name, shape, dt in [
            ("xin", [HID, TPC], F32),
            ("qwgu", [NCH, 128, 512], I32),
            ("sgu_rows", [NCH, 4096], F32),
            ("sgu_exp", [NCH, 128, 4096], BF16),
            ("sgu_g", [32, NCH * 128], F32),
            ("qzgu", [32, 2752], I32),
            ("qwdn", [128, NPAIR * 512], I32),
            ("sdn_rows", [len(DN_KCS) * 8, 4096], F32),
            ("sdn_exp", [len(DN_KCS) * 8, 128, 4096], BF16),
            ("sdn_g", [NPAIR, 4096], F32),
            ("qzdn", [NPAIR, 512], I32),
            ("wq", [8, 128, 4096], F32),
            ("bq8", [128, 8], F32),
            ("m8", [8, 128, 128], F32),
            ("z8", [128, 8], F32),
            ("wo", [32, 128, 4096], F32),
            ("ones_col", [128, 1], F32),
            ("ones_row", [1, 128], F32),
            ("ident", [128, 128], F32)]:
        ap[name] = nc.dram_tensor(name, shape, dt, kind="ExternalInput").ap()
    xout = nc.dram_tensor("xout", [HID, TPC], F32, kind="ExternalOutput").ap()
    zgu_rows = nc.dram_tensor("zgu_rows", [NCH, 4096], U8, kind="Internal").ap()
    zdn_rows = nc.dram_tensor("zdn_rows", [len(DN_KCS) * 8, 4096], U8,
                              kind="Internal").ap()
    zgu_exp = nc.dram_tensor("zgu_exp", [128, NCH * 4096], U8,
                             kind="Internal").ap()
    zdn_exp = nc.dram_tensor("zdn_exp", [128, len(DN_KCS) * 8 * 4096], U8,
                             kind="Internal").ap()

    stack = ExitStack()
    with stack:
        b = B(nc, stack)
        sb = b.sb
        y = sb("y", 128, NSLAB * 1024, BF16)
        bq8 = sb("bq8s", 128, 8, F32)
        z8 = sb("z8s", 128, 8, BF16)
        m8 = sb("m8s", 128, 8 * 128, BF16)
        identb = sb("identb", 128, 128, BF16)
        identf = sb("identf", 128, 128, F32)
        onec = sb("onec", 128, 1, BF16)
        onerf = sb("onerf", 1, 128, F32)
        srow = sb("srow", 1, 4096, BF16)
        srow2 = sb("srow2", 1, 4096, BF16)
        srow2z = sb("srow2z", 1, 4096, U8)
        srowdz = sb("srowdz", 1, 4096, U8)
        mrow = sb("mrow", 1, 1024, F32)
        rrow = sb("rrow", 1, 1024, F32)
        statc = sb("statc", 128, 64, F32)
        arena_h = sb("arena", 128, 106496, U8)
        arena = arena_h[:]

        def carve(off, nbytes, dt, p=128):
            return arena[:, off:off + nbytes].bitcast(dt)[0:p, :]

        qwg = carve(0, 4096, I32)
        wi8 = carve(4096, 4096, U8)
        wdqg = carve(8192, 16384, BF16)
        sbg = carve(24576, 8192, BF16)
        hbuf = carve(32768, 16384, BF16)
        qwd = carve(49152, 4096, I32)
        wdqd = carve(53248, 16384, BF16)
        sbd = carve(69632, 16384, BF16)
        dnst = carve(86016, 4096, BF16)
        zbg = carve(90112, 4096, U8)
        silu_sb = carve(94208, 4096, BF16)
        xsl = carve(53248, 8192, F32)
        xbfs = carve(61440, 4096, BF16)
        zbd = carve(102400, 4096, U8)
        xsqs = carve(65536, 4096, BF16)
        wsub = carve(98304, 4096, I8)
        ln_tmp = carve(86016, 4096, F32)
        sq = carve(0, 16384, BF16)
        retr = carve(16384, 16384, BF16)
        wtile = carve(32768, 16384, BF16)
        qscr = carve(49152, 8192, F32)
        rcp = carve(57344, 512, F32)
        b.fbuf = sb("fbuf", 1, 8, F32)

        pA = stack.enter_context(nc.psum_tensor("pA", [128, 1024], F32))
        pB = stack.enter_context(nc.psum_tensor("pB", [128, 1024], F32))
        pC = stack.enter_context(nc.psum_tensor("pC", [128, 1024], F32))
        pD = stack.enter_context(nc.psum_tensor("pD", [128, 1024], F32))

        acc_total = [0]
        v_acc_region = {}
        evac_slot = [0, 0]
        dnst_free = [None, None]

        b.opn("vector", lambda e: e.memset(b.fbuf[0:1, 0:8], 0.0))
        v_d2d = b.dma("sync", "ld", xout, ap["xin"])
        b.dma("gpsimd", "cst", onec[:], ap["ones_col"])
        b.dma("sync", "ld", onerf[:], ap["ones_row"])
        b.dma("gpsimd", "cst", identb[:], ap["ident"])
        b.dma("sync", "ld", identf[:], ap["ident"])
        b.dma("sync", "ld", bq8[:], ap["bq8"])
        b.dma("gpsimd", "cst", z8[:], ap["z8"])
        v_m8 = 0
        for h in range(8):
            v_m8 = b.dma("gpsimd", "cst", m8[:, h * 128:(h + 1) * 128], ap["m8"][h])

        qzg_sb = carve(53248, 16384, I32, p=32)
        vz = b.dma("sync", "ld", qzg_sb[:, 0:2752], ap["qzgu"])
        zb = carve(0, 32768, U8)
        qzv = qzg_sb.bitcast(U8)
        b.wait("vector", "ld", vz)
        vu = 0
        for gate in range(2):
            base = gate * 5504
            in_v = qzv[0:32, base:base + 86 * 64].rearrange(
                "g (i bb) -> g i bb", bb=64)
            for hi in range(2):
                out_v = zb[0:32, 0:22016].rearrange(
                    "g (i blk sub) -> g i blk sub", blk=4, sub=64)[
                    :, :, gate * 2 + hi, :]
                if hi == 0:
                    vu = b.op("vector", "dv",
                              lambda e, o=out_v, iv=in_v: e.tensor_scalar(
                                  out=o, in0=iv, scalar1=15, scalar2=None,
                                  op0=OP.bitwise_and))
                else:
                    vu = b.op("vector", "dv",
                              lambda e, o=out_v, iv=in_v: e.tensor_scalar(
                                  out=o, in0=iv, scalar1=4, scalar2=15,
                                  op0=OP.logical_shift_right,
                                  op1=OP.bitwise_and))
        vu = b.fence("vector", "dv")
        b.wait("sync", "dv", vu)
        v_store = 0
        for q in range(4):
            v_store = b.dma("sync", "ld",
                            zgu_rows.rearrange("c (g p) -> g c p", g=32)[
                                q * 8:(q + 1) * 8, :, :],
                            zb[q * 8:(q + 1) * 8, 0:22016].rearrange(
                                "g (c p) -> g c p", p=128))
        qzd_sb = carve(53248, 16384, I32, p=NPAIR)
        vz3 = b.dma("sync", "ld", qzd_sb[:, 0:512], ap["qzdn"])
        b.wait("vector", "ld", vz3)
        b.wait("vector", "ld", v_store)
        qzv2 = qzd_sb[:].bitcast(U8)
        vu = 0
        for hi in range(2):
            in_v = qzv2[0:NPAIR, 0:2048].rearrange("g (hc bb) -> g hc bb", bb=64)
            out_v = zb[0:NPAIR, 0:4096].rearrange(
                "g (hc blk sub) -> g hc blk sub", blk=2, sub=64)[:, :, hi, :]
            if hi == 0:
                vu = b.op("vector", "dv", lambda e, o=out_v, iv=in_v: e.tensor_scalar(
                    out=o, in0=iv, scalar1=15, scalar2=None, op0=OP.bitwise_and))
            else:
                vu = b.op("vector", "dv", lambda e, o=out_v, iv=in_v: e.tensor_scalar(
                    out=o, in0=iv, scalar1=4, scalar2=15,
                    op0=OP.logical_shift_right, op1=OP.bitwise_and))
        v_init_done = b.fence("vector", "dv")
        b.wait("sync", "dv", v_init_done)
        s0 = 0
        v_ztab = 0
        for kc, ns in enumerate(DN_KCS):
            v_ztab = b.dma("sync", "ld",
                           bass.AP(zdn_rows.tensor, kc * 8 * 4096,
                                   [[512, ns], [4096, 8], [128, 4], [1, 128]]),
                           zb[s0:s0 + ns, 0:4096].rearrange("g (hc p) -> g hc p", p=128))
            s0 += ns
        b.wait("scalar", "ld", v_ztab)
        NGU = NCH * 4096
        NDN = len(DN_KCS) * 8 * 4096
        b.dma("scalar", "dbl", zgu_exp[0:1, :],
              bass.AP(zgu_rows.tensor, 0, [[NGU, 1], [1, NGU]]))
        b.dma("scalar", "dbl", zdn_exp[0:1, :],
              bass.AP(zdn_rows.tensor, 0, [[NDN, 1], [1, NDN]]))
        p = 1
        while p < 128:
            b.wait("scalar", "dbl", b.sems["dbl"].n)
            b.dma("scalar", "dbl", zgu_exp[p:2 * p, :], zgu_exp[0:p, :])
            b.dma("scalar", "dbl", zdn_exp[p:2 * p, :], zdn_exp[0:p, :])
            p *= 2
        v_zexp = b.sems["dbl"].n

        def emit_ln(src_dram, want_Y):
            pe_done = [0, 0]
            v_xb = [0, 0]
            v_xq = [0, 0]
            b.barrier("sync", ["acc", "ac0", "ac1", "ac2", "ac3"])
            b.barrier("sync", ["ld", "dv", "ac"])
            b.barrier("tensor", ["dv", "ac"])
            b.barrier("vector", ["pe", "ac"])
            b.barrier("scalar", ["pe", "dv"])
            for g in range(NSLAB):
                bu = g % 2
                b.wait("sync", "dv", v_xb[bu])
                b.wait("sync", "ac", v_xq[bu])
                vld = b.dma("sync", "ld", xsl[:, bu * 1024:(bu + 1) * 1024],
                            src_dram[g * 128:(g + 1) * 128, :])
                b.wait("vector", "ld", vld)
                b.wait("vector", "pe", pe_done[bu])
                v_xb[bu] = b.opf("vector", "dv", lambda e, bu=bu: e.tensor_copy(
                    out=xbfs[:, bu * 1024:(bu + 1) * 1024],
                    in_=xsl[:, bu * 1024:(bu + 1) * 1024]))
                b.wait("scalar", "ld", vld)
                b.wait("scalar", "pe", pe_done[bu])
                v_xq[bu] = b.opf("scalar", "ac", lambda e, bu=bu: e.activation(
                    out=xsqs[:, bu * 1024:(bu + 1) * 1024],
                    in_=xsl[:, bu * 1024:(bu + 1) * 1024], func=AF.Square))
                b.wait("tensor", "dv", v_xb[bu])
                b.wait("tensor", "ac", v_xq[bu])
                for t in range(8):
                    b.opn("tensor", lambda e, t=t, g=g, bu=bu: e.matmul(
                        pC[:, t * 64 + g:t * 64 + g + 1],
                        xbfs[:, bu * 1024 + t * 128:bu * 1024 + (t + 1) * 128],
                        onec[:], start=True, stop=True))
                for t in range(8):
                    if t < 7:
                        b.opn("tensor", lambda e, t=t, g=g, bu=bu: e.matmul(
                            pC[:, t * 64 + 32 + g:t * 64 + 32 + g + 1],
                            xsqs[:, bu * 1024 + t * 128:bu * 1024 + (t + 1) * 128],
                            onec[:], start=True, stop=True))
                    else:
                        pe_done[bu] = b.op("tensor", "pe",
                                           lambda e, t=t, g=g, bu=bu: e.matmul(
                                               pC[:, t * 64 + 32 + g:t * 64 + 33 + g],
                                               xsqs[:, bu * 1024 + t * 128:
                                                    bu * 1024 + (t + 1) * 128],
                                               onec[:], start=True, stop=True))
            v_stats_pe = pe_done[(NSLAB - 1) % 2]
            b.wait("vector", "pe", v_stats_pe)
            vv = 0
            for t in range(8):
                vv = b.op("vector", "dv", lambda e, t=t: e.reduce_sum(
                    out=statc[:, t:t + 1], in_=pC[:, t * 64:t * 64 + 32], axis=AX.X))
                vv = b.op("vector", "dv", lambda e, t=t: e.reduce_sum(
                    out=statc[:, 8 + t:9 + t], in_=pC[:, t * 64 + 32:t * 64 + 64],
                    axis=AX.X))
            b.wait("vector", "dv", vv)
            vv = b.op("vector", "dv", lambda e: e.tensor_scalar(
                out=statc[:, 16:24], in0=statc[:, 0:8], scalar1=1.0 / HID,
                scalar2=None, op0=OP.mult))
            vv = b.op("vector", "dv", lambda e: e.tensor_scalar(
                out=statc[:, 24:32], in0=statc[:, 0:8], scalar1=1.0 / 32.0,
                scalar2=None, op0=OP.mult))
            b.wait("vector", "dv", vv)
            vv = b.op("vector", "dv", lambda e: e.tensor_mul(
                out=statc[:, 32:40], in0=statc[:, 16:24], in1=statc[:, 16:24]))
            b.wait("vector", "dv", vv)
            vv = b.op("vector", "dv", lambda e: e.scalar_tensor_tensor(
                out=statc[:, 40:48], in0=statc[:, 8:16], scalar=1.0 / HID,
                in1=statc[:, 32:40], op0=OP.mult, op1=OP.subtract))
            b.wait("vector", "dv", vv)
            vv = b.opf("vector", "dv", lambda e: e.tensor_scalar(
                out=statc[:, 40:48], in0=statc[:, 40:48], scalar1=1e-5,
                scalar2=None, op0=OP.add))
            b.wait("scalar", "dv", vv)
            va = b.opf("scalar", "ac", lambda e: e.activation(
                out=statc[:, 48:56], in_=statc[:, 40:48], func=AF.Ln))
            b.wait("vector", "ac", va)
            vv = b.opf("vector", "dv", lambda e: e.tensor_scalar(
                out=statc[:, 48:56], in0=statc[:, 48:56], scalar1=-0.5,
                scalar2=None, op0=OP.mult))
            b.wait("scalar", "dv", vv)
            va = b.opf("scalar", "ac", lambda e: e.activation(
                out=statc[:, 56:64], in_=statc[:, 48:56], func=AF.Exp))
            b.wait("tensor", "dv", b.sems["dv"].n)
            b.wait("tensor", "ac", b.sems["ac"].n)
            for which, col0, row_t, ps in ((0, 16, mrow, pA), (1, 56, rrow, pB)):
                vp = 0
                for t in range(8):
                    if which == 1:
                        b.wait("tensor", "ac", va)
                    vp = b.op("tensor", "pe", lambda e, t=t, col0=col0: e.transpose(
                        pD[0:1, t * 128:(t + 1) * 128],
                        statc[:, col0 + t:col0 + t + 1], identf[:]))
                b.wait("vector", "pe", vp)
                vv = b.opf("vector", "dv", lambda e, row_t=row_t: e.tensor_copy(
                    out=row_t[:], in_=pD[0:1, 0:1024]))
                b.wait("tensor", "dv", vv)
                for hf in range(2):
                    vbc = b.op("tensor", "pe", lambda e, ps=ps, row_t=row_t, hf=hf:
                               e.matmul(ps[:, hf * 512:(hf + 1) * 512], onerf[:],
                                        row_t[0:1, hf * 512:(hf + 1) * 512],
                                        start=True, stop=True))
            v_yw = 0
            v_sub = [v_xb[0], v_xb[1]]
            b.wait("sync", "ac", max(v_xq))
            for g in range(NSLAB):
                bu = g % 2
                b.wait("sync", "dv", v_sub[bu])
                vld = b.dma("sync", "ld", xsl[:, bu * 1024:(bu + 1) * 1024],
                            src_dram[g * 128:(g + 1) * 128, :])
                b.wait("vector", "ld", vld)
                if g == 0:
                    b.wait("vector", "pe", vbc)
                v_sub[bu] = b.opf("vector", "dv", lambda e, bu=bu: e.tensor_sub(
                    out=ln_tmp[:], in0=xsl[:, bu * 1024:(bu + 1) * 1024],
                    in1=pA[:]))
                v_yw = b.opf("vector", "dv", lambda e, g=g: e.tensor_mul(
                    out=y[:, g * 1024:(g + 1) * 1024], in0=ln_tmp[:],
                    in1=pB[:]))
            return v_yw

        def emit_ffn(src_x):
            v_y_done = emit_ln(src_x, want_Y=True)
            if tap == "y":
                for g in range(NSLAB):
                    b.wait("gpsimd", "dv", v_y_done)
                    va = b.dma("gpsimd", "acc", xout[g * 128:(g + 1) * 128, :],
                               y[:, g * 1024:(g + 1) * 1024])
                    acc_total[0] = va
                return
            if tap == "pc":
                vv = b.op("vector", "dv", lambda e: e.tensor_copy(
                    out=qscr[:, 0:1024], in_=pC[:, 0:512]) if False else e.tensor_copy(
                    out=qscr[:, 0:512], in_=pC[:, 0:512]))
                vv = b.op("vector", "dv", lambda e: e.tensor_copy(
                    out=qscr[:, 512:1024], in_=pC[:, 512:1024]))
                b.wait("gpsimd", "dv", vv)
                va = b.dma("gpsimd", "acc", xout[256:384, :], qscr[:, 0:1024])
                va = b.dma("gpsimd", "acc", xout[384:512, :], xbfs[:, 0:1024])
                va = b.dma("gpsimd", "acc", xout[512:640, :], xsl[:, 0:1024])
                acc_total[0] = va
                return
            if tap == "stats":
                b.wait("gpsimd", "dv", v_y_done)
                va = b.dma("gpsimd", "acc", xout[0:32, :], Yy[:])
                va = b.dma("gpsimd", "acc", xout[128:256, 0:64], statc[:])
                acc_total[0] = va
                return
            v_deq = [0] * (NCH + 2)
            v_unp = [0] * (NCH + 2)
            v_mm = [0] * (NCH + 1)
            v_silu = [0] * (NPAIR + 1)
            v_hTT = [0] * (NPAIR + 1)
            v_qwld = [0] * (NCH + 2)
            v_sb_ok = [0] * (NCH + 2)
            v_srow = [0] * (NCH + 2)
            v_tt = [0] * (NCH + 2)
            v_sub_done = [0] * (NCH + 2)
            v_zrow = [0] * (NCH + 2)
            v_zb_ok = [0] * (NCH + 2)
            v_dnq = [[0] * 8 for _ in DN_KCS]
            dn_off = []
            _o = 0
            for _ns in DN_KCS:
                dn_off.append(_o)
                _o += _ns * 512
            v_dndeq = [[0] * 8 for _ in DN_KCS]
            v_dnsub = [[0] * 8 for _ in DN_KCS]
            v_dnzrow = [[0] * 8 for _ in DN_KCS]
            v_dnzb = [[0] * 8 for _ in DN_KCS]
            v_dnmm = [[0] * 64 for _ in DN_KCS]
            v_dnsb = [[0] * 8 for _ in DN_KCS]
            v_dnsrow = [[0] * 8 for _ in DN_KCS]


            def load_qw(c):
                if c > NCH - 1:
                    return
                b.wait("sync", "dv", v_tt[c - 2] if c >= 2 else 0)
                v_qwld[c] = b.dma("sync", "ld",
                                  qwg[:, (c % 2) * 512:(c % 2) * 512 + 512],
                                  ap["qwgu"][c])

            def bcast_s(c):
                if c > NCH - 1:
                    return
                b.wait("sync", "dv", v_tt[c - 1] if c >= 1 else 0)
                v_sb_ok[c] = b.dma("sync", "ld", sbg[:], ap["sgu_exp"][c])
                if c == 0:
                    b.wait("sync", "dbl", v_zexp)
                b.wait("sync", "dv", v_sub_done[c - 1] if c >= 1 else 0)
                v_zb_ok[c] = b.dma("sync", "ld", zbg[:],
                                   zgu_exp[:, c * 4096:(c + 1) * 4096])

            def dequant_gu(c):
                if c > NCH - 1:
                    return
                par = c % 2
                b.wait("vector", "ld", v_qwld[c])
                b.wait("vector", "ld", v_sb_ok[c])
                b.wait("vector", "ld", v_zb_ok[c])
                b.wait("vector", "pe", v_mm[c - 2] if c >= 2 else 0)
                qv = qwg[:, par * 512:(par + 1) * 512].bitcast(U8).rearrange(
                    "p (g bb) -> p g bb", bb=64)
                o = wi8[:].rearrange("p (g blk sub) -> p g blk sub", blk=2, sub=64)
                b.op("vector", "dv", lambda e: e.tensor_scalar(
                    out=o[:, :, 0, :], in0=qv, scalar1=15, scalar2=None,
                    op0=OP.bitwise_and))
                v_unp[c] = b.op("vector", "dv", lambda e: e.tensor_scalar(
                    out=o[:, :, 1, :], in0=qv, scalar1=4, scalar2=15,
                    op0=OP.logical_shift_right, op1=OP.bitwise_and))
                b.wait("vector", "dv", v_unp[c])
                v_sub_done[c] = b.op("vector", "dv", lambda e: e.tensor_sub(
                    out=wsub[:], in0=wi8[:], in1=zbg[:]))
                b.wait("vector", "dv", v_sub_done[c])
                v_tt[c] = b.opf("vector", "dv", lambda e, par=par: e.tensor_mul(
                    out=wdqg[:, par * 4096:(par + 1) * 4096], in0=wsub[:],
                    in1=sbg[:]))
                v_deq[c] = v_tt[c]

            def mm_chunk(c):
                par = c % 2
                i = c // 2
                is_gate = (c % 2 == 0)
                ps = pA if is_gate else pB
                b.wait("tensor", "dv", v_deq[c])
                if c <= 1:
                    b.wait("tensor", "dv", v_y_done)
                if is_gate:
                    b.wait("tensor", "ac", v_silu[i - 1] if i >= 1 else 0)
                else:
                    b.wait("tensor", "dv", v_hTT[i - 1] if i >= 1 else 0)
                for hf in range(2):
                    for k in range(NSLAB):
                        fn = (lambda e, k=k, hf=hf, par=par, ps=ps:
                              e.matmul(ps[:, hf * 512:(hf + 1) * 512],
                                       wdqg[:, par * 4096 + k * 128:
                                            par * 4096 + (k + 1) * 128],
                                       y[:, k * 1024 + hf * 512:
                                         k * 1024 + hf * 512 + 512],
                                       start=(k == 0), stop=(k == NSLAB - 1)))
                        if hf == 1 and k == NSLAB - 1:
                            v_mm[c] = b.op("tensor", "pe", fn)
                        else:
                            b.opn("tensor", fn)
                if is_gate:
                    b.wait("scalar", "pe", v_mm[c])
                    b.wait("scalar", "dv", v_hTT[i - 2] if i >= 2 else 0)
                    v_silu[i] = b.opf("scalar", "ac", lambda e, i=i: e.activation(
                        out=silu_sb[:, (i % 2) * 1024:(i % 2) * 1024 + 1024],
                        in_=pA[:], func=AF.Silu))
                else:
                    b.wait("vector", "pe", v_mm[c])
                    b.wait("vector", "ac", v_silu[i])
                    kc = i // 8
                    if kc >= 1:
                        b.wait("vector", "pe", v_dnmm[kc - 1][-1])
                    v_hTT[i] = b.opf("vector", "dv", lambda e, i=i: e.tensor_mul(
                        out=hbuf[:, (i % 8) * 1024:(i % 8) * 1024 + 1024],
                        in0=silu_sb[:, (i % 2) * 1024:(i % 2) * 1024 + 1024],
                        in1=pB[:]))

            def load_qwdn(kc, hcg):
                if kc > len(DN_KCS) - 1 or hcg > 7:
                    return
                ns = DN_KCS[kc]
                gi = kc * 8 + hcg
                slot = gi % 2
                if gi >= 2:
                    pk, ph = divmod(gi - 2, 8)
                    b.wait("sync", "dv", v_dndeq[pk][ph])
                off = dn_off[kc] + hcg * ns * 64
                v_dnq[kc][hcg] = b.dma("sync", "ld", qwd[:, slot * 512:slot * 512 + ns * 64],
                                       ap["qwdn"][:, off:off + ns * 64])

            def dn_bcast(kc, hcg):
                ns = DN_KCS[kc]
                par = hcg % 2
                if hcg >= 2:
                    b.wait("sync", "dv", v_dndeq[kc][hcg - 2])
                elif kc >= 1:
                    b.wait("sync", "dv", v_dndeq[kc - 1][6 + hcg])
                v_dnsb[kc][hcg] = b.dma(
                    "sync", "ld", sbd[:, par * 4096:par * 4096 + ns * 512],
                    ap["sdn_exp"][kc * 8 + hcg][:, 0:ns * 512])
                if hcg >= 1:
                    b.wait("sync", "dv", v_dnsub[kc][hcg - 1])
                elif kc >= 1:
                    b.wait("sync", "dv", v_dnsub[kc - 1][7])
                ns512 = DN_KCS[kc] * 512
                off = (kc * 8 + hcg) * 4096
                v_dnzb[kc][hcg] = b.dma("sync", "ld", zbd[:, 0:ns512],
                                        zdn_exp[:, off:off + ns512])

            def dn_dequant(kc, hcg):
                ns = DN_KCS[kc]
                par = hcg % 2
                b.wait("vector", "ld", v_dnq[kc][hcg])
                b.wait("vector", "ld", v_dnsb[kc][hcg])
                b.wait("vector", "ld", v_dnzb[kc][hcg])
                if hcg >= 2:
                    b.wait("vector", "pe", v_dnmm[kc][(hcg - 1) * 8 - 1])
                elif kc >= 1:
                    b.wait("vector", "pe", v_dnmm[kc - 1][(7 + hcg - 1) * 8 - 1])
                slot = (kc * 8 + hcg) % 2
                qv = qwd[:, slot * 512:slot * 512 + ns * 64].bitcast(U8).rearrange(
                    "p (g hc bb) -> p g hc bb", hc=4, bb=64)
                o = wi8[:, 0:ns * 512].rearrange(
                    "p (g hc blk sub) -> p g hc blk sub", hc=4, blk=2, sub=64)
                b.op("vector", "dv", lambda e: e.tensor_scalar(
                    out=o[:, :, :, 0, :], in0=qv, scalar1=15, scalar2=None,
                    op0=OP.bitwise_and))
                vv = b.op("vector", "dv", lambda e: e.tensor_scalar(
                    out=o[:, :, :, 1, :], in0=qv, scalar1=4, scalar2=15,
                    op0=OP.logical_shift_right, op1=OP.bitwise_and))
                b.wait("vector", "dv", vv)
                v_dnsub[kc][hcg] = b.op("vector", "dv",
                                        lambda e, ns=ns: e.tensor_sub(
                                            out=wsub[:, 0:ns * 512],
                                            in0=wi8[:, 0:ns * 512],
                                            in1=zbd[:, 0:ns * 512]))
                b.wait("vector", "dv", v_dnsub[kc][hcg])
                v_dndeq[kc][hcg] = b.opf("vector", "dv",
                                         lambda e, par=par, ns=ns: e.tensor_mul(
                                             out=wdqd[:, par * 4096:par * 4096 + ns * 512],
                                             in0=wsub[:, 0:ns * 512],
                                             in1=sbd[:, par * 4096:par * 4096 + ns * 512]))

            def dn_phase2(kc):
                ns = DN_KCS[kc]
                last = False
                if tap == "dn3" and kc == 3:
                    b.wait("gpsimd", "dv", v_hTT[31])
                    for sl in range(8):
                        va = b.dma("gpsimd", "acc", xout[sl * 128:(sl + 1) * 128, :],
                                   hbuf[:, sl * 1024:(sl + 1) * 1024])
                    acc_total[0] = va
                    tap_stop[0] = True
                    return
                if tap == "dn0" and kc == 0:
                    b.wait("gpsimd", "dv", v_hTT[7])
                    for sl in range(8):
                        va = b.dma("gpsimd", "acc", xout[sl * 128:(sl + 1) * 128, :],
                                   hbuf[:, sl * 1024:(sl + 1) * 1024])
                    acc_total[0] = va
                    tap_stop[0] = True
                    return
                for ht in range(32):
                    hcg = ht // 4
                    hc = ht % 4
                    par = hcg % 2
                    if hc == 0:
                        dn_bcast(kc, hcg)
                        dn_dequant(kc, hcg)
                        gi = kc * 8 + hcg + 2
                        load_qwdn(gi // 8, gi % 8)
                    b.wait("tensor", "dv", v_dndeq[kc][hcg])
                    for hf in range(2):
                        piece = ht * 2 + hf
                        slot = piece % 2
                        b.wait("tensor", "ac", evac_slot[slot])
                        for g in range(ns):
                            fn = (lambda e, g=g, hc=hc, hf=hf, par=par, slot=slot,
                                  stop=(g == ns - 1):
                                  e.matmul(pC[:, slot * 512:slot * 512 + 512],
                                           wdqd[:, par * 4096 + g * 512 + hc * 128:
                                                par * 4096 + g * 512 + (hc + 1) * 128],
                                           hbuf[:, g * 1024 + hf * 512:
                                                g * 1024 + hf * 512 + 512],
                                           start=(g == 0), stop=stop))
                            if g == ns - 1:
                                v_dnmm[kc][piece] = b.op("tensor", "pe", fn)
                            else:
                                b.opn("tensor", fn)
                        if tap == "dnp" and kc == 0 and ht == 0 and hf == 1:
                            b.wait("vector", "pe", v_dnmm[0][1])
                            vv = b.opf("vector", "dv", lambda e: e.tensor_copy(
                                out=ln_tmp[:], in_=pC[:]))
                            b.wait("gpsimd", "dv", vv)
                            va = b.dma("gpsimd", "acc", xout[0:128, :], ln_tmp[:])
                            b.wait("vector", "acc", va)
                            for sl in range(8):
                                va = b.dma("gpsimd", "acc",
                                           xout[(1 + sl) * 128:(2 + sl) * 128, 0:512],
                                           wdqd[:, sl * 512:sl * 512 + 512])
                            for q in range(4):
                                vv = b.opf("vector", "dv", lambda e, q=q: e.tensor_copy(
                                    out=ln_tmp[:], in_=wi8[:, q * 1024:(q + 1) * 1024]))
                                b.wait("gpsimd", "dv", vv)
                                va = b.dma("gpsimd", "acc",
                                           xout[(9 + q) * 128:(10 + q) * 128, :],
                                           ln_tmp[:])
                                b.wait("vector", "acc", va)
                                vv = b.opf("vector", "dv", lambda e, q=q: e.tensor_copy(
                                    out=ln_tmp[:], in_=sbd[:, q * 1024:(q + 1) * 1024]))
                                b.wait("gpsimd", "dv", vv)
                                va = b.dma("gpsimd", "acc",
                                           xout[(13 + q) * 128:(14 + q) * 128, :],
                                           ln_tmp[:])
                                b.wait("vector", "acc", va)
                            acc_total[0] = va
                            tap_stop[0] = True
                            return
                        b.wait("scalar", "pe", v_dnmm[kc][piece])
                        stg = ht % 2
                        if hf == 0 and dnst_free[stg]:
                            b.wait("scalar", *dnst_free[stg])
                        b.opn("scalar", lambda e, stg=stg, hf=hf, slot=slot:
                              e.activation(out=dnst[:, stg * 1024 + hf * 512:
                                                    stg * 1024 + hf * 512 + 512],
                                           in_=pC[:, slot * 512:slot * 512 + 512],
                                           func=AF.Copy, scale=0.5))
                        vev = b.fence("scalar", "ac")
                        evac_slot[slot] = vev
                        if hf == 1:
                            cls = "ac%d" % (ht % 4)
                            b.wait("gpsimd", "ac", vev)
                            b.wait("gpsimd", "ld", v_d2d)
                            pr = v_acc_region.get(ht)
                            if pr:
                                b.wait("gpsimd", pr[0], pr[1])
                            va = b.dma("gpsimd", cls,
                                       xout[ht * 128:(ht + 1) * 128, :],
                                       dnst[:, stg * 1024:stg * 1024 + 1024],
                                       accum_op=OP.add)
                            v_acc_region[ht] = (cls, va)
                            dnst_free[stg] = (cls, va)

            tap_stop = [False]
            b.wait("sync", "ld", v_ztab)
            load_qw(0)
            load_qw(1)
            bcast_s(0)
            load_qwdn(0, 0)
            load_qwdn(0, 1)
            dequant_gu(0)
            if tap == "dq0":
                b.wait("vector", "dv", v_tt[0])
                for nm, src_ap, rows in (
                        ("wi8", wi8[:].bitcast(U8), 4),
                        ("zbg", zbg, 4),
                        ("wsub", wsub, 4),
                        ("wdqg", wdqg[:, 0:4096], 4),
                        ("sbg", sbg, 4)):
                    pass
                va = 0
                for q in range(4):
                    vv = b.opf("vector", "dv", lambda e, q=q: e.tensor_copy(
                        out=ln_tmp[:], in_=wi8[:, q * 1024:(q + 1) * 1024]))
                    b.wait("gpsimd", "dv", vv)
                    va = b.dma("gpsimd", "acc", xout[q * 128:(q + 1) * 128, :], ln_tmp[:])
                    b.wait("vector", "acc", va)
                    vv = b.opf("vector", "dv", lambda e, q=q: e.tensor_copy(
                        out=ln_tmp[:], in_=zbg[:, q * 1024:(q + 1) * 1024]))
                    b.wait("gpsimd", "dv", vv)
                    va = b.dma("gpsimd", "acc", xout[(4 + q) * 128:(5 + q) * 128, :], ln_tmp[:])
                    b.wait("vector", "acc", va)
                    vv = b.opf("vector", "dv", lambda e, q=q: e.tensor_copy(
                        out=ln_tmp[:], in_=wsub[:, q * 1024:(q + 1) * 1024]))
                    b.wait("gpsimd", "dv", vv)
                    va = b.dma("gpsimd", "acc", xout[(8 + q) * 128:(9 + q) * 128, :], ln_tmp[:])
                    b.wait("vector", "acc", va)
                    vv = b.opf("vector", "dv", lambda e, q=q: e.tensor_copy(
                        out=ln_tmp[:], in_=wdqg[:, q * 1024:(q + 1) * 1024]))
                    b.wait("gpsimd", "dv", vv)
                    va = b.dma("gpsimd", "acc", xout[(12 + q) * 128:(13 + q) * 128, :], ln_tmp[:])
                    b.wait("vector", "acc", va)
                    vv = b.opf("vector", "dv", lambda e, q=q: e.tensor_copy(
                        out=ln_tmp[:], in_=sbg[:, q * 1024:(q + 1) * 1024]))
                    b.wait("gpsimd", "dv", vv)
                    va = b.dma("gpsimd", "acc", xout[(16 + q) * 128:(17 + q) * 128, :], ln_tmp[:])
                    b.wait("vector", "acc", va)
                for r in range(4):
                    va = b.dma("gpsimd", "acc", xout[2560 + r:2561 + r, 0:1024],
                               zgu_rows[r:r + 1, 0:1024])
                acc_total[0] = va
                return
            bcast_s(1)
            dequant_gu(1)
            for c in range(NCH):
                load_qw(c + 2)
                bcast_s(c + 2)
                mm_chunk(c)
                if tap in ("gu_a", "gu_b") and c == 0:
                    b.wait("vector", "pe", v_mm[0])
                    vv = b.opf("vector", "dv", lambda e: e.tensor_copy(
                        out=ln_tmp[:], in_=pA[:]))
                    b.wait("gpsimd", "dv", vv)
                    va = b.dma("gpsimd", "acc", xout[0:128, :], ln_tmp[:])
                    b.wait("vector", "acc", va)
                    for q in range(4):
                        vv = b.opf("vector", "dv", lambda e, q=q: e.tensor_copy(
                            out=ln_tmp[:], in_=wdqg[:, q * 1024:(q + 1) * 1024]))
                        b.wait("gpsimd", "dv", vv)
                        va = b.dma("gpsimd", "acc",
                                   xout[(1 + q) * 128:(2 + q) * 128, :], ln_tmp[:])
                        b.wait("vector", "acc", va)
                        vv = b.opf("vector", "dv", lambda e, q=q: e.tensor_copy(
                            out=ln_tmp[:], in_=zbg[:, q * 1024:(q + 1) * 1024]))
                        b.wait("gpsimd", "dv", vv)
                        va = b.dma("gpsimd", "acc",
                                   xout[(5 + q) * 128:(6 + q) * 128, :], ln_tmp[:])
                        b.wait("vector", "acc", va)
                    acc_total[0] = va
                    return
                dequant_gu(c + 2)
                if tap == "dq2" and c == 0:
                    b.wait("vector", "dv", v_tt[2])
                    va = 0
                    for q in range(4):
                        vv = b.opf("vector", "dv", lambda e, q=q: e.tensor_copy(
                            out=ln_tmp[:], in_=wi8[:, q * 1024:(q + 1) * 1024]))
                        b.wait("gpsimd", "dv", vv)
                        va = b.dma("gpsimd", "acc", xout[q * 128:(q + 1) * 128, :], ln_tmp[:])
                        b.wait("vector", "acc", va)
                        vv = b.opf("vector", "dv", lambda e, q=q: e.tensor_copy(
                            out=ln_tmp[:], in_=zbg[:, q * 1024:(q + 1) * 1024]))
                        b.wait("gpsimd", "dv", vv)
                        va = b.dma("gpsimd", "acc", xout[(4 + q) * 128:(5 + q) * 128, :], ln_tmp[:])
                        b.wait("vector", "acc", va)
                        vv = b.opf("vector", "dv", lambda e, q=q: e.tensor_copy(
                            out=ln_tmp[:], in_=wdqg[:, q * 1024:(q + 1) * 1024]))
                        b.wait("gpsimd", "dv", vv)
                        va = b.dma("gpsimd", "acc", xout[(8 + q) * 128:(9 + q) * 128, :], ln_tmp[:])
                        b.wait("vector", "acc", va)
                        vv = b.opf("vector", "dv", lambda e, q=q: e.tensor_copy(
                            out=ln_tmp[:], in_=sbg[:, q * 1024:(q + 1) * 1024]))
                        b.wait("gpsimd", "dv", vv)
                        va = b.dma("gpsimd", "acc", xout[(12 + q) * 128:(13 + q) * 128, :], ln_tmp[:])
                        b.wait("vector", "acc", va)
                    acc_total[0] = va
                    tap_stop[0] = True
                    return
                if tap == "gu0" and c == 1:
                    vv = b.opf("vector", "dv", lambda e: e.tensor_copy(
                        out=qscr[:, 0:1024], in_=pB[:]))
                    vv = b.opf("vector", "dv", lambda e: e.tensor_copy(
                        out=ln_tmp[:], in_=pA[:]))
                    b.wait("gpsimd", "dv", max(vv, v_hTT[0]))
                    b.wait("gpsimd", "ac", v_silu[0])
                    va = b.dma("gpsimd", "acc", xout[0 * 128:1 * 128, :],
                               wdqg[:, 0:1024])
                    va = b.dma("gpsimd", "acc", xout[1 * 128:2 * 128, :],
                               hbuf[:, 0:1024])
                    va = b.dma("gpsimd", "acc", xout[2 * 128:3 * 128, :],
                               qscr[:, 0:1024])
                    va = b.dma("gpsimd", "acc", xout[3 * 128:4 * 128, :],
                               silu_sb[:, 0:1024])
                    va = b.dma("gpsimd", "acc", xout[4 * 128:5 * 128, :],
                               ln_tmp[:])
                    va = b.dma("gpsimd", "acc", xout[5 * 128:6 * 128, :],
                               y[:, 0:1024])
                    va = b.dma("gpsimd", "acc", xout[6 * 128:6 * 128 + 32, :],
                               ngc[0:32, 0:1024])
                    va = b.dma("gpsimd", "acc", xout[7 * 128:7 * 128 + 32, :],
                               Yy[:])
                    for q in range(4):
                        va = b.dma("gpsimd", "acc",
                                   xout[(8 + q) * 128:(9 + q) * 128, :],
                                   wdqg[:, q * 1024:(q + 1) * 1024])
                    acc_total[0] = va
                    return
                i = c // 2
                if c % 2 == 1 and ((i + 1) % 8 == 0 or i == NPAIR - 1):
                    kc = i // 8
                    dn_phase2(kc)
                    if tap in ("dnacc", "dnacc3") and kc == (0 if tap == "dnacc" else 3):
                        return
                    if tap_stop[0]:
                        return
            return

        def emit_attn():
            v_y2 = emit_ln(xout, want_Y=False)
            v_wq = [0] * 10
            v_qmm = [0] * 9
            v_bias = [0] * 9
            v_sq = [0] * 9
            v_den = [0] * 9
            for h in range(8):
                par = h % 2
                if h >= 2:
                    b.wait("gpsimd", "pe", v_qmm[h - 2])
                v_wq[h] = b.dma("gpsimd", "cst",
                                wtile[:, par * 4096:(par + 1) * 4096], ap["wq"][h])
                b.wait("tensor", "cst", v_wq[h])
                if h == 0:
                    b.wait("tensor", "dv", v_y2)
                if h >= 1:
                    b.wait("tensor", "dv", v_bias[h - 1])
                for hf in range(2):
                    for k in range(NSLAB):
                        fn = (lambda e, k=k, hf=hf, par=par: e.matmul(
                            pA[:, hf * 512:(hf + 1) * 512],
                            wtile[:, par * 4096 + k * 128:par * 4096 + (k + 1) * 128],
                            y[:, k * 1024 + hf * 512:k * 1024 + hf * 512 + 512],
                            start=(k == 0), stop=(k == NSLAB - 1)))
                        if hf == 1 and k == NSLAB - 1:
                            v_qmm[h] = b.op("tensor", "pe", fn)
                        else:
                            b.opn("tensor", fn)
                b.wait("vector", "pe", v_qmm[h])
                v_bias[h] = b.opf("vector", "dv", lambda e, h=h: e.tensor_scalar(
                    out=qscr[:, 0:1024], in0=pA[:], scalar1=bq8[:, h:h + 1],
                    scalar2=None, op0=OP.add))
                b.wait("vector", "dv", v_bias[h])
                vv = b.opf("vector", "dv", lambda e: e.tensor_scalar(
                    out=qscr[:, 1024:2048], in0=qscr[:, 0:1024], scalar1=0.0,
                    scalar2=None, op0=OP.min))
                b.wait("scalar", "dv", vv)
                va = b.opf("scalar", "ac", lambda e: e.activation(
                    out=silu_sb[:, 0:1024].bitcast(F32)[:, 0:512] if False else
                    silu_sb[:].bitcast(F32)[:, 0:1024],
                    in_=qscr[:, 1024:2048], func=AF.Exp))
                b.wait("vector", "ac", va)
                vv = b.op("vector", "dv", lambda e: e.tensor_scalar(
                    out=qscr[:, 1024:2048], in0=qscr[:, 0:1024], scalar1=0.0,
                    scalar2=None, op0=OP.max))
                b.wait("vector", "dv", vv)
                v_sq[h] = b.opf("vector", "dv", lambda e, h=h: e.tensor_add(
                    out=sq[:, h * 1024:(h + 1) * 1024],
                    in0=silu_sb[:].bitcast(F32)[:, 0:1024],
                    in1=qscr[:, 1024:2048]))
                b.wait("tensor", "dv", v_sq[h])
                for t in range(8):
                    fn = (lambda e, t=t, h=h: e.matmul(
                        pD[:, t * 8 + h:t * 8 + h + 1],
                        sq[:, h * 1024 + t * 128:h * 1024 + (t + 1) * 128],
                        z8[:, h:h + 1], start=True, stop=True))
                    if t == 7:
                        v_den[h] = b.op("tensor", "pe", fn)
                    else:
                        b.opn("tensor", fn)
            b.wait("vector", "pe", v_den[7])
            vv = b.op("vector", "dv", lambda e: e.tensor_scalar(
                out=rcp[:, 0:64], in0=pD[:, 0:64], scalar1=1e-6, scalar2=None,
                op0=OP.add))
            b.wait("vector", "dv", vv)
            v_rcp = b.opf("vector", "dv", lambda e: e.reciprocal(
                out=rcp[:, 64:128], in_=rcp[:, 0:64]))
            v_nummm = [0] * 9
            v_retrev = [0] * 9
            v_sqn = [0] * 9
            for h in range(8):
                b.wait("tensor", "dv", v_rcp)
                vp = 0
                for t in range(8):
                    vp = b.op("tensor", "pe", lambda e, t=t, h=h: e.transpose(
                        pD[0:1, t * 128:(t + 1) * 128],
                        rcp[:, 64 + t * 8 + h:64 + t * 8 + h + 1], identf[:]))
                b.wait("vector", "pe", vp)
                vv = b.opf("vector", "dv", lambda e: e.tensor_copy(
                    out=rrow[:], in_=pD[0:1, 0:1024]))
                b.wait("tensor", "dv", vv)
                vb = 0
                for hf in range(2):
                    vb = b.op("tensor", "pe", lambda e, hf=hf: e.matmul(
                        pB[:, hf * 512:(hf + 1) * 512], onerf[:],
                        rrow[0:1, hf * 512:(hf + 1) * 512], start=True, stop=True))
                b.wait("vector", "pe", vb)
                if h >= 2:
                    b.wait("vector", "pe", v_nummm[h - 2])
                v_sqn[h] = b.opf("vector", "dv", lambda e, h=h: e.tensor_mul(
                    out=silu_sb[:, (h % 2) * 1024:(h % 2) * 1024 + 1024],
                    in0=sq[:, h * 1024:(h + 1) * 1024], in1=pB[:]))
                b.wait("tensor", "dv", v_sqn[h])
                b.wait("tensor", "cst", v_m8)
                if h >= 1:
                    b.wait("tensor", "dv", v_retrev[h - 1])
                for hf in range(2):
                    fn = (lambda e, hf=hf, h=h: e.matmul(
                        pA[:, hf * 512:(hf + 1) * 512],
                        m8[:, h * 128:(h + 1) * 128],
                        silu_sb[:, (h % 2) * 1024 + hf * 512:
                                (h % 2) * 1024 + hf * 512 + 512],
                        start=True, stop=True))
                    if hf == 1:
                        v_nummm[h] = b.op("tensor", "pe", fn)
                    else:
                        b.opn("tensor", fn)
                b.wait("vector", "pe", v_nummm[h])
                v_retrev[h] = b.opf("vector", "dv", lambda e, h=h: e.tensor_copy(
                    out=retr[:, h * 1024:(h + 1) * 1024], in_=pA[:]))
            v_retr = b.sems["dv"].n
            v_wo = [0] * 34
            v_att = [0] * 66

            for nt in range(32):
                par = nt % 2
                if nt >= 2:
                    b.wait("gpsimd", "pe", v_att[(nt - 2) * 2 + 1])
                v_wo[nt] = b.dma("gpsimd", "cst",
                                 wtile[:, par * 4096:(par + 1) * 4096], ap["wo"][nt])
                b.wait("tensor", "cst", v_wo[nt])
                b.wait("tensor", "dv", v_retr)
                for hf in range(2):
                    slot = (nt * 2 + hf) % 2
                    b.wait("tensor", "ac", evac_slot[slot])
                    for k in range(32):
                        fn = (lambda e, k=k, hf=hf, par=par, slot=slot: e.matmul(
                            pC[:, slot * 512:slot * 512 + 512],
                            wtile[:, par * 4096 + k * 128:par * 4096 + (k + 1) * 128],
                            retr[:, (k // 4) * 1024 + hf * 512:
                                 (k // 4) * 1024 + hf * 512 + 512],
                            start=(k == 0), stop=(k == 31)))
                        if k == 31:
                            v_att[nt * 2 + hf] = b.op("tensor", "pe", fn)
                        else:
                            b.opn("tensor", fn)
                    b.wait("scalar", "pe", v_att[nt * 2 + hf])
                    stg = nt % 2
                    if hf == 0 and dnst_free[stg]:
                        b.wait("scalar", *dnst_free[stg])
                    b.opn("scalar", lambda e, stg=stg, hf=hf, slot=slot:
                          e.activation(out=dnst[:, stg * 1024 + hf * 512:
                                                stg * 1024 + hf * 512 + 512],
                                       in_=pC[:, slot * 512:slot * 512 + 512],
                                       func=AF.Copy, scale=1.0))
                    vev = b.fence("scalar", "ac")
                    evac_slot[slot] = vev
                    if hf == 1:
                        cls = "ac%d" % (nt % 4)
                        b.wait("gpsimd", "ac", vev)
                        pr = v_acc_region.get(nt)
                        if pr:
                            b.wait("gpsimd", pr[0], pr[1])
                        va = b.dma("gpsimd", cls, xout[nt * 128:(nt + 1) * 128, :],
                                   dnst[:, stg * 1024:stg * 1024 + 1024],
                                   accum_op=OP.add)
                        v_acc_region[nt] = (cls, va)
                        dnst_free[stg] = (cls, va)

        b.wait("vector", "dv", v_init_done)
        if n_sub >= 1:
            emit_ffn(ap["xin"])
        if n_sub >= 2:
            emit_attn()
        if n_sub >= 3:
            emit_ffn(xout)
        b.barrier("gpsimd", ["acc", "ac0", "ac1", "ac2", "ac3"])
        if n_sub == 0:
            b.wait("gpsimd", "ld", v_d2d)

        with nc.Block() as block:
            @block.sync
            def _(sync):
                for fn in b.q["sync"]:
                    fn(sync)

            @block.gpsimd
            def _(gp):
                for fn in b.q["gpsimd"]:
                    fn(gp)

            @block.scalar
            def _(sc):
                for fn in b.q["scalar"]:
                    fn(sc)

            @block.vector
            def _(ve):
                for fn in b.q["vector"]:
                    fn(ve)

            @block.tensor
            def _(te):
                for fn in b.q["tensor"]:
                    fn(te)

    return nc


def kernel(**inputs):
    w = stage_weights(inputs)
    x = np.asarray(inputs["L0_attn_hidden_states"], dtype=np.float32)
    x2 = x.reshape(NCORES * TPC, HID)[:, PERMH]
    xT = np.ascontiguousarray(x2.T)

    nc = build_nc(N_SUBLAYERS)
    in_maps = []
    for c in range(NCORES):
        m = dict(w)
        m["xin"] = np.ascontiguousarray(xT[:, c * TPC:(c + 1) * TPC])
        in_maps.append(m)
    res = run_bass_kernel_spmd(nc, in_maps, core_ids=list(range(NCORES)))
    out = np.concatenate([r["xout"] for r in res.results], axis=1)
    full = np.empty((NCORES * TPC, HID), dtype=np.float32)
    full[:, PERMH] = out.T
    return full.reshape(2, 4096, HID)
